# version 26
# baseline (speedup 1.0000x reference)
import numpy as np
from contextlib import ExitStack

import concourse.bass as bass
import concourse.mybir as mybir
from concourse.bass_utils import run_bass_kernel_spmd

F32 = mybir.dt.float32
BF16 = mybir.dt.bfloat16
I32 = mybir.dt.int32
AF = mybir.ActivationFunctionType
ALU = mybir.AluOpType

D = 1024
SEQ = 2048
NSEQ = 2
HID = 3072
NDC = 8
NHG = 8
NKC = 16
NSP = 4
NSP8 = 8
NHB = 12
EPS = 1e-5
QI = float(1.0 / 127.0)
ISEQ = float(1.0 / 2048.0)
ID = float(1.0 / 1024.0)

ENGINES = ("tensor", "vector", "scalar", "sync", "gpsimd")


class TView:

    def __init__(self, t, off, n):
        self.t = t
        self.off = off
        self.n = n

    def __getitem__(self, idx):
        p, c = idx
        a = 0 if c.start is None else c.start
        b = self.n if c.stop is None else c.stop
        return self.t[p, self.off + a:self.off + b]


class PB:

    def __init__(self, nc, es):
        self.nc = nc
        self.ops = {e: [] for e in ENGINES}
        self.sems = {}
        for n in ("T", "V", "S", "P", "g0", "g1", "g2", "g3", "y0", "y1",
                  "bT", "bV", "bS"):
            self.sems[n] = [es.enter_context(nc.semaphore(n)), 0]
        self.gj = 0
        self.yj = 0
        self.war_v = 0
        self.war_s = 0
        self.trace = []

    def cnt(self, n):
        return self.sems[n][1]

    def emit(self, eng, fn=None, wait=(), inc=None, incd=1):
        wl = [(self.sems[n][0], v) for (n, v) in wait if v > 0]
        post = None
        if inc is not None:
            self.sems[inc][1] += incd
            post = (self.sems[inc][0], incd)

        def run(e, wl=wl, fn=fn, post=post):
            for h, v in wl:
                e.wait_ge(h, v)
            ins = fn(e) if fn is not None else None
            if post is not None:
                if ins is None:
                    ins = e.nop()
                ins.then_inc(post[0], post[1])

        import traceback as _tb
        cal = _tb.extract_stack(limit=4)[0]
        self.trace.append((eng, [(n, v) for (n, v) in wait if v > 0],
                           (inc, incd) if inc else None,
                           "%s:%d" % (cal.filename.split("/")[-1], cal.lineno)))
        self.ops[eng].append(run)
        return self.cnt(inc) if inc is not None else None

    def simulate(self):
        idx = {e: 0 for e in ENGINES}
        sems = {n: 0 for n in self.sems}
        per = {e: [t for t in self.trace if t[0] == e] for e in ENGINES}
        progress = True
        while progress:
            progress = False
            for e in ENGINES:
                while idx[e] < len(per[e]):
                    eng, waits, inc, where = per[e][idx[e]]
                    if all(sems[n] >= v for n, v in waits):
                        if inc:
                            sems[inc[0]] += inc[1]
                        idx[e] += 1
                        progress = True
                    else:
                        break
        stuck = {e: (idx[e], len(per[e])) for e in ENGINES if idx[e] < len(per[e])}
        if stuck:
            print("DEADLOCK: stuck engines:", stuck)
            print("sem state:", sems)
            for e in stuck:
                eng, waits, inc, where = per[e][idx[e]]
                bad = [(n, v, sems[n]) for n, v in waits if sems[n] < v]
                print("  %s stuck at op %d (%s) waiting %s" % (e, idx[e], where, bad))
        else:
            print("simulate: plan completes; ops:", {e: len(per[e]) for e in ENGINES})
        return stuck

    def gdma(self, fn, wait=()):
        lane = "g%d" % (self.gj % 4)
        use = self.gj // 4
        self.gj += 1
        w = list(wait)
        if use >= 1:
            w.append((lane, 16 * use))
        v = self.emit("gpsimd", fn, wait=w, inc=lane, incd=16)
        return (lane, v)

    def ydma(self, fn, wait=()):
        lane = "y%d" % (self.yj % 2)
        use = self.yj // 2
        self.yj += 1
        w = list(wait)
        if use >= 1:
            w.append((lane, 16 * use))
        v = self.emit("sync", fn, wait=w, inc=lane, incd=16)
        return (lane, v)

    def barrier(self):
        t = self.emit("tensor", None, inc="bT")
        v = self.emit("vector", None, inc="bV")
        s = self.emit("scalar", None, inc="bS")
        self.emit("tensor", None, wait=(("bV", v), ("bS", s)))
        self.emit("vector", None, wait=(("bT", t), ("bS", s)))
        self.emit("scalar", None, wait=(("bT", t), ("bV", v)))
        return self.cnt("T"), self.cnt("V"), self.cnt("S")


def build_program(upto="all", taps=()):
    nc = bass.Bass("TRN2", target_bir_lowering=False)
    taps = set(taps)

    xT = nc.dram_tensor("xT", [D, NSEQ * SEQ], F32, kind="ExternalInput").ap()
    yT = nc.dram_tensor("yT", [D, NSEQ * SEQ], F32, kind="ExternalOutput").ap()
    w1u = nc.dram_tensor("w1u", [D, D], F32, kind="ExternalInput").ap()
    w1g = nc.dram_tensor("w1g", [D, NHG * 384], F32, kind="ExternalInput").ap()
    w2 = nc.dram_tensor("w2", [D, D], F32, kind="ExternalInput").ap()
    b1u = nc.dram_tensor("b1u", [128, 8], F32, kind="ExternalInput").ap()
    b1q = nc.dram_tensor("b1q", [128, 8], F32, kind="ExternalInput").ap()
    b1k = nc.dram_tensor("b1k", [128, 8], F32, kind="ExternalInput").ap()
    b1vr = nc.dram_tensor("b1vr", [128, D], F32, kind="ExternalInput").ap()
    b2l = nc.dram_tensor("b2l", [128, 8], F32, kind="ExternalInput").ap()
    gaml = nc.dram_tensor("gaml", [128, 8], F32, kind="ExternalInput").ap()
    betl = nc.dram_tensor("betl", [128, 8], F32, kind="ExternalInput").ap()
    wgt = nc.dram_tensor("wgt", [D, HID], I32, kind="ExternalInput").ap()
    wut = nc.dram_tensor("wut", [D, HID], I32, kind="ExternalInput").ap()
    wdt = nc.dram_tensor("wdt", [HID, D], I32, kind="ExternalInput").ap()
    tap_t = {}
    for t in taps:
        shp = {"xn": [128, 16384], "u": [128, 16384], "kt0": [128, 2048],
               "vt0": [128, 2048], "m": [128, 16384], "x1": [128, 16384],
               "xn2": [128, 2048], "an": [128, 2048],
               "abc": [128, 1024], "ra": [1, 512], "rb": [1, 512],
               "sq": [128, 1024]}[t]
        tap_t[t] = nc.dram_tensor("tap_" + t, shp, F32, kind="ExternalOutput").ap()

    es = ExitStack()
    pb = PB(nc, es)

    B1U = es.enter_context(nc.sbuf_tensor("B1U", [128, 8], F32))
    B1Q = es.enter_context(nc.sbuf_tensor("B1Q", [128, 8], F32))
    B1K = es.enter_context(nc.sbuf_tensor("B1K", [128, 8], F32))
    B1VR = es.enter_context(nc.sbuf_tensor("B1VR", [128, D], BF16))
    B2L = es.enter_context(nc.sbuf_tensor("B2L", [128, 8], F32))
    GAM = es.enter_context(nc.sbuf_tensor("GAM", [128, 8], F32))
    BET = es.enter_context(nc.sbuf_tensor("BET", [128, 8], F32))
    ONESC = es.enter_context(nc.sbuf_tensor("ONESC", [128, 1], BF16))
    CEPS = es.enter_context(nc.sbuf_tensor("CEPS", [128, 1], F32))
    CQI = es.enter_context(nc.sbuf_tensor("CQI", [128, 1], F32))
    ONESR = es.enter_context(nc.sbuf_tensor("ONESR", [1, 128], F32))
    ONESRB = es.enter_context(nc.sbuf_tensor("ONESRB", [1, 128], BF16))
    AX = es.enter_context(nc.sbuf_tensor("AX", [128, 16384], BF16))
    R0 = es.enter_context(nc.sbuf_tensor("R0", [1, 512], F32))
    R1 = es.enter_context(nc.sbuf_tensor("R1", [1, 512], F32))
    RA = es.enter_context(nc.sbuf_tensor("RA", [1, 512], F32))
    RB = es.enter_context(nc.sbuf_tensor("RB", [1, 512], F32))
    SQ = es.enter_context(nc.sbuf_tensor("SQ", [128, 1024], BF16))
    ABC = es.enter_context(nc.sbuf_tensor("ABC", [128, 1024], BF16))
    OUTB = es.enter_context(nc.sbuf_tensor("OUTB", [128, 1024], F32))
    ARENA = es.enter_context(nc.sbuf_tensor("ARENA", [128, 79360], BF16))
    PSUMA = es.enter_context(nc.psum_tensor("PSUMA", [128, 4096], F32))

    pb.gdma(lambda g: g.dma_start(B1U[:, :], b1u))
    pb.gdma(lambda g: g.dma_start(B1Q[:, :], b1q))
    pb.gdma(lambda g: g.dma_start(B1K[:, :], b1k))
    pb.gdma(lambda g: g.dma_start(B1VR[:, :], b1vr))
    pb.gdma(lambda g: g.dma_start(B2L[:, :], b2l))
    pb.gdma(lambda g: g.dma_start(GAM[:, :], gaml))
    pb.gdma(lambda g: g.dma_start(BET[:, :], betl))
    consts_w = [(n, pb.cnt(n)) for n in ("g0", "g1", "g2", "g3")]
    pb.emit("vector", lambda v: v.memset(ONESC[:, :], 1.0), inc="V")
    _vc = pb.emit("vector", lambda v: v.memset(CEPS[:, :], 100000.0), inc="V")
    _vc = pb.emit("vector", lambda v: v.reciprocal(CEPS[:, :], CEPS[:, :]),
                  wait=(("V", _vc),), inc="V")
    _vc = pb.emit("vector", lambda v: v.memset(CQI[:, :], 127.0),
                  wait=(("V", _vc),), inc="V")
    _vc = pb.emit("vector", lambda v: v.reciprocal(CQI[:, :], CQI[:, :]),
                  wait=(("V", _vc),), inc="V")
    pb.emit("vector", lambda v: v.memset(ONESR[:, :], 1.0), inc="V")
    pb.emit("vector", lambda v: v.memset(ONESRB[:, :], 1.0), inc="V")
    pb.emit("scalar", None, wait=consts_w)
    pb.emit("vector", None, wait=consts_w)
    pb.emit("tensor", None, wait=consts_w)

    xT_v = xT.rearrange("(c p) t -> p c t", p=128)
    w1u_v = w1u.rearrange("(c p) f -> p c f", p=128)
    w1g_v = w1g.rearrange("(c p) f -> p c f", p=128)
    w2_v = w2.rearrange("(c p) f -> p c f", p=128)
    wgt_v = wgt.rearrange("(c p) h -> p c h", p=128)
    wut_v = wut.rearrange("(c p) h -> p c h", p=128)
    wdt_v = wdt.rearrange("(c p) d -> p c d", p=128)

    AXv = lambda dc, lo, w: AX[:, dc * 2048 + lo:dc * 2048 + lo + w]

    def ln_stats(P, W, src, t_src_waits, v_src_waits):
        SX, SXX, ABCP = P["SX"], P["SXX"], P["ABCP"]
        sq_t = P.setdefault("sq_t", [0, 0])
        t_last = 0
        for dc in range(NDC):
            h = dc % 2
            w = list(v_src_waits) if dc == 0 else []
            if dc < 2 and pb.war_v:
                w.append(("V", pb.war_v))
            if sq_t[h]:
                w.append(("T", sq_t[h]))
            v_sq = pb.emit("vector", lambda vv, dc=dc, h=h: vv.tensor_tensor(
                SQ[:, h * 512:h * 512 + W], src(dc), src(dc), ALU.mult),
                wait=w, inc="V")
            w = list(t_src_waits) if dc == 0 else []
            if dc == 0 and P.get("sx_reuse_v", 0):
                w.append(("V", P["sx_reuse_v"]))
            pb.emit("tensor", lambda t, dc=dc: t.matmul(
                SX[0:1, 0:W], ONESC[:, :], src(dc),
                start=(dc == 0), stop=(dc == NDC - 1)), wait=w)
            t_last = pb.emit("tensor", lambda t, dc=dc, h=h: t.matmul(
                SXX[0:1, 0:W], ONESC[:, :], SQ[:, h * 512:h * 512 + W],
                start=(dc == 0 and not P.get("packed")), stop=(dc == NDC - 1)),
                wait=(("V", v_sq),), inc="T")
            sq_t[h] = t_last
        pb.emit("vector", lambda v: v.tensor_scalar_mul(R0[0:1, 0:W], SX[0:1, 0:W], -ID),
                wait=(("T", t_last), ("S", pb.war_s)))
        vr = pb.emit("vector", lambda v: v.tensor_scalar_mul(R1[0:1, 0:W], SXX[0:1, 0:W], ID), inc="V")
        vm = pb.emit("vector", lambda v: v.tensor_tensor(RM[0:1, 0:W], R0[0:1, 0:W], R0[0:1, 0:W], ALU.mult),
                     wait=(("V", vr),), inc="V")
        v_rows = pb.emit("vector", lambda v: v.tensor_tensor(
            R1[0:1, 0:W], R1[0:1, 0:W], RM[0:1, 0:W], ALU.subtract),
            wait=(("V", vm),), inc="V")
        sl = pb.emit("scalar", lambda s2: s2.activation(RM[0:1, 0:W], R1[0:1, 0:W], AF.Ln, bias=CEPS[0:1, 0:1]),
                     wait=(("V", v_rows),), inc="S")
        s_rows = pb.emit("scalar", lambda s2: s2.activation(RA[0:1, 0:W], RM[0:1, 0:W], AF.Exp, scale=-0.5),
                         wait=(("S", sl),), inc="S")
        pb.war_s = s_rows
        v_rb = pb.emit("vector", lambda v: v.tensor_tensor(
            RB[0:1, 0:W], R0[0:1, 0:W], RA[0:1, 0:W], ALU.mult),
            wait=(("S", s_rows),), inc="V")
        w = [("V", v_rb)]
        if P.get("abc_reuse_v", 0):
            w.append(("V", P["abc_reuse_v"]))
        pb.emit("tensor", lambda t: t.matmul(ABCP[:, 0:W], ONESR[:, :], RA[0:1, 0:W],
                                             start=True, stop=True), wait=w)
        t_bc = pb.emit("tensor", lambda t: t.matmul(ABCP[:, W:2 * W], ONESR[:, :], RB[0:1, 0:W],
                                                    start=True, stop=True), inc="T")
        v_abc = pb.emit("vector", lambda v: v.tensor_copy(ABC[:, 0:2 * W], ABCP[:, 0:2 * W]),
                        wait=(("T", t_bc),), inc="V")
        P["abc_reuse_v"] = v_abc
        P["sx_reuse_v"] = v_abc
        return v_abc, t_bc

    def ln_stats_steps(P, W, src, t_src_waits, v_src_waits):
        SX, SXX, ABCP = P["SX"], P["SXX"], P["ABCP"]
        sq_eng = P.get("sq_eng", "vector")
        abc_eng = P.get("abc_eng", "vector")
        st = {"t_last": 0}

        def chunk(dc):
            def go():
                sq_t = P.setdefault("sq_t", [0, 0])
                h = dc % 2
                w = list(v_src_waits) if dc == 0 else []
                if dc == 4:
                    w.extend(P.get("mid_waits", ()))
                if dc < 2 and pb.war_v and not P.get("no_war_v"):
                    w.append(("V", pb.war_v))
                if sq_t[h]:
                    w.append(("T", sq_t[h]))
                if sq_eng == "scalar":
                    e_sq = ("S", pb.emit("scalar", lambda s2, dc=dc, h=h: s2.square(
                        SQ[:, h * 512:h * 512 + W], src(dc)),
                        wait=w, inc="S"))
                else:
                    e_sq = ("V", pb.emit("vector", lambda vv, dc=dc, h=h: vv.tensor_tensor(
                        SQ[:, h * 512:h * 512 + W], src(dc), src(dc), ALU.mult),
                        wait=w, inc="V"))
                w = list(t_src_waits) if dc == 0 else []
                if dc == 4:
                    w.extend(P.get("mid_waits", ()))
                if dc == 0 and P.get("sx_reuse", None):
                    w.append(P["sx_reuse"])
                pb.emit("tensor", lambda t, dc=dc: t.matmul(
                    SX[0:1, 0:W], ONESC[:, :], src(dc),
                    start=(dc == 0), stop=(dc == NDC - 1)), wait=w)
                st["t_last"] = pb.emit("tensor", lambda t, dc=dc, h=h: t.matmul(
                    SXX[0:1, 0:W], ONESC[:, :], SQ[:, h * 512:h * 512 + W],
                    start=(dc == 0 and not P.get("packed")), stop=(dc == NDC - 1)),
                    wait=(e_sq,), inc="T")
                P["sq_t"][h] = st["t_last"]
            return go

        def rows1():
            pb.emit("vector", lambda v: v.tensor_scalar_mul(R0[0:1, 0:W], SX[0:1, 0:W], -ID),
                    wait=(("T", st["t_last"]), ("S", pb.war_s)))
            vr = pb.emit("vector", lambda v: v.tensor_scalar_mul(R1[0:1, 0:W], SXX[0:1, 0:W], ID), inc="V")
            vm = pb.emit("vector", lambda v: v.tensor_tensor(RB[0:1, 0:W], R0[0:1, 0:W], R0[0:1, 0:W], ALU.mult),
                         wait=(("V", vr),), inc="V")
            st["v_rows"] = pb.emit("vector", lambda v: v.tensor_tensor(
                R1[0:1, 0:W], R1[0:1, 0:W], RB[0:1, 0:W], ALU.subtract),
                wait=(("V", vm),), inc="V")

        def rows2():
            sl = pb.emit("scalar", lambda s2: s2.activation(RB[0:1, 0:W], R1[0:1, 0:W], AF.Ln, bias=CEPS[0:1, 0:1]),
                         wait=(("V", st["v_rows"]),), inc="S")
            s_rows = pb.emit("scalar", lambda s2: s2.activation(RA[0:1, 0:W], RB[0:1, 0:W], AF.Exp, scale=-0.5),
                             wait=(("S", sl),), inc="S")
            pb.war_s = s_rows
            st["v_rb"] = pb.emit("vector", lambda v: v.tensor_tensor(
                RB[0:1, 0:W], R0[0:1, 0:W], RA[0:1, 0:W], ALU.mult),
                wait=(("S", s_rows),), inc="V")

        def bcast():
            w = [("V", st["v_rb"])]
            if P.get("abc_reuse", None):
                w.append(P["abc_reuse"])
            pb.emit("tensor", lambda t: t.matmul(ABCP[:, 0:W], ONESR[:, :], RA[0:1, 0:W],
                                                 start=True, stop=True), wait=w)
            t_bc = pb.emit("tensor", lambda t: t.matmul(ABCP[:, W:2 * W], ONESR[:, :], RB[0:1, 0:W],
                                                        start=True, stop=True), inc="T")
            wc = [("T", t_bc)]
            wc.extend(P.get("abc_readers", ()))
            if abc_eng == "scalar":
                e_abc = ("S", pb.emit("scalar", lambda s2: s2.copy(
                    ABC[:, 0:2 * W], ABCP[:, 0:2 * W]), wait=wc, inc="S"))
            else:
                e_abc = ("V", pb.emit("vector", lambda v: v.tensor_copy(
                    ABC[:, 0:2 * W], ABCP[:, 0:2 * W]), wait=wc, inc="V"))
            P["abc_reuse"] = e_abc
            P["sx_reuse"] = e_abc
            P["abc_ready"] = e_abc
        return [chunk(dc) for dc in range(NDC)] + [rows1, rows2, bcast]

    out_toks = []
    stop = False
    for s in range(NSEQ):
        if stop:
            break
        _o = 0
        def _mk(n):
            nonlocal _o
            v = TView(ARENA, _o, n)
            _o += n
            return v
        XBF = _mk(16384)
        U = _mk(16384)
        WBIG = _mk(8192)
        KT = _mk(4096)
        VT = _mk(4096)
        WQKV = _mk(6144)
        QSP = _mk(1024)
        ATT = _mk(2048)
        AN = _mk(4096)
        VTMP = _mk(512)

        bar = pb.barrier()
        bw = [("T", bar[0]), ("V", bar[1]), ("S", bar[2])]
        bwx = bw if s > 0 else []

        def x_half(sp, hh, w):
            return pb.gdma(lambda g, sp=sp, hh=hh, s=s: g.dma_start(
                XBF[:, :].rearrange("p (c t) -> p c t", c=NDC)
                [:, hh * 4:hh * 4 + 4, sp * 512:(sp + 1) * 512],
                xT_v[:, hh * 4:hh * 4 + 4, s * SEQ + sp * 512:s * SEQ + (sp + 1) * 512]),
                wait=w)
        x_tok = [[x_half(0, 0, bwx), x_half(0, 1, bwx)]]
        w1u_tok = pb.gdma(lambda g: g.dma_start(
            WBIG[:, :].rearrange("p (c f) -> p c f", f=1024), w1u_v), wait=bwx)
        qkv_tok_g0 = pb.gdma(lambda g: g.dma_start(
            WQKV[:, 0:3072].rearrange("p (c f) -> p c f", f=384),
            w1g_v[:, :, 0:384]))
        for sp in range(1, NSP):
            x_tok.append([x_half(sp, 0, ()), x_half(sp, 1, ())])

        if upto == "load":
            pb.gdma(lambda sy: sy.dma_start(tap_t["xn"], XBF[:, :]),
                    wait=[t for pr in x_tok for t in pr])
            stop = True
        if stop:
            continue
        SP = TView(PSUMA, 0, 2048)
        AVP = TView(PSUMA, 2048, 1024)
        PP2 = TView(PSUMA, 3072, 1024)
        sp_evac_s = [0, 0]
        att_t = [0, 0]
        avp_v = [0, 0]
        pp2_use = [0, 0]
        pp2_eng = ["S", "S"]
        pp2_i = [0]
        wqkv_t = [0, 0]
        ktvt_t = [0, 0]
        kt_done = [0, 0]
        vt_done = [0, 0]
        qsp_t = 0
        m_v = 0
        av_i = 0

        def pp2_lane():
            lane = pp2_i[0] % 2
            pp2_i[0] += 1
            return lane

        def kt_tile_steps(hg, lw, sp, qkv_tok, extra_w=()):
            bf_ = hg % 2
            lane = pp2_lane()
            st = {"tl": 0}

            def mk_mm(dc):
                def go():
                    w = []
                    if dc == 0:
                        w = [qkv_tok] + list(extra_w)
                        if pp2_use[lane]:
                            w.append((pp2_eng[lane], pp2_use[lane]))
                        if ktvt_t[bf_]:
                            w.append(("T", ktvt_t[bf_]))
                    r = pb.emit("tensor", lambda t, sp=sp, dc=dc, lane=lane, lw=lw: t.matmul(
                        PP2[:, lane * 512:(lane + 1) * 512],
                        WQKV[:, lw * 3072 + dc * 384 + 128:lw * 3072 + dc * 384 + 256],
                        AXv(dc, sp * 512, 512),
                        start=(dc == 0), stop=(dc == NDC - 1)),
                        wait=w, inc="T" if dc == NDC - 1 else None)
                    if r is not None:
                        st["tl"] = r
                return go

            def evac():
                kt_done[bf_] = pb.emit("scalar", lambda s2, hg=hg, sp=sp, lane=lane, bf_=bf_: s2.activation(
                    KT[:, bf_ * 2048 + sp * 512:bf_ * 2048 + (sp + 1) * 512],
                    PP2[:, lane * 512:(lane + 1) * 512],
                    AF.Silu, bias=B1K[:, hg:hg + 1]),
                    wait=(("T", st["tl"]),), inc="S")
                pp2_use[lane] = kt_done[bf_]
                pp2_eng[lane] = "S"
            steps = [mk_mm(dc) for dc in range(NDC)]
            steps[-1] = (lambda last=steps[-1]: (last(), evac()))
            return steps

        def vt_tile_steps(hg, lw, tb, qkv_tok, extra_w=()):
            bf_ = hg % 2
            lane = pp2_lane()
            st = {"tl": 0}

            def mk_mm(tci, dc):
                tc = tb * 4 + tci

                def go():
                    w = []
                    if tci == 0 and dc == 0:
                        w = [qkv_tok] + list(extra_w)
                        if pp2_use[lane]:
                            w.append((pp2_eng[lane], pp2_use[lane]))
                        if ktvt_t[bf_]:
                            w.append(("T", ktvt_t[bf_]))
                    pb.emit("tensor", lambda t, tc=tc, tci=tci, dc=dc, lane=lane, lw=lw: t.matmul(
                        PP2[:, lane * 512 + tci * 128:lane * 512 + (tci + 1) * 128],
                        AXv(dc, tc * 128, 128),
                        WQKV[:, lw * 3072 + dc * 384 + 256:lw * 3072 + dc * 384 + 384],
                        start=(dc == 0), stop=False),
                        wait=w)
                    if dc == NDC - 1:
                        r = pb.emit("tensor", lambda t, tci=tci, lane=lane, hg=hg: t.matmul(
                            PP2[:, lane * 512 + tci * 128:lane * 512 + (tci + 1) * 128],
                            ONESRB[0:1, 0:128],
                            B1VR[0:1, hg * 128:(hg + 1) * 128],
                            start=False, stop=True),
                            inc="T" if tci == 3 else None)
                        if r is not None:
                            st["tl"] = r
                return go

            def evac():
                vt_done[bf_] = pb.emit("scalar", lambda s2, tb=tb, bf_=bf_, lane=lane: s2.activation(
                    VT[:, bf_ * 2048 + tb * 512:bf_ * 2048 + (tb + 1) * 512],
                    PP2[:, lane * 512:(lane + 1) * 512], AF.Silu),
                    wait=(("T", st["tl"]),), inc="S")
                pp2_use[lane] = vt_done[bf_]
                pp2_eng[lane] = "S"
            steps = []
            for tci in range(4):
                for dc in range(0, NDC, 2):
                    steps.append((lambda a=mk_mm(tci, dc), b=mk_mm(tci, dc + 1): (a(), b())))
            steps[-1] = (lambda last=steps[-1]: (last(), evac()))
            return steps


        P1 = {
            "SX": TView(PSUMA, 0, 512),
            "SXX": TView(PSUMA, 512, 512),
            "ABCP": TView(PSUMA, 1024, 1024),
        }
        PPs = (TView(PSUMA, 2048, 512), TView(PSUMA, 2560, 512))
        pp_s = [0, 0]
        pp_i = [0]
        u_s = 0
        t_u_last = 0
        xn_v = 0
        xn_v_sp = [0] * NSP

        def u_tile(uc, sp):
            nonlocal u_s, t_u_last
            lane = pp_i[0] % 2
            pp_i[0] += 1
            w = [("V", xn_v_sp[sp])]
            if uc == 0 and sp == 0:
                w.append(w1u_tok)
            if pp_s[lane]:
                w.append(("S", pp_s[lane]))
            for dc in range(NDC):
                r = pb.emit("tensor", lambda t, uc=uc, sp=sp, dc=dc, lane=lane: t.matmul(
                    PPs[lane][:, :], WBIG[:, dc * 1024 + uc * 128:dc * 1024 + (uc + 1) * 128],
                    AXv(dc, sp * 512, 512),
                    start=(dc == 0), stop=(dc == NDC - 1)),
                    wait=w if dc == 0 else (), inc="T" if dc == NDC - 1 else None)
                if r is not None:
                    t_u_last = r
            u_s = pb.emit("scalar", lambda s2, uc=uc, sp=sp, lane=lane: s2.activation(
                U[:, uc * 2048 + sp * 512:uc * 2048 + (sp + 1) * 512], PPs[lane][:, :],
                AF.Silu, bias=B1U[:, uc:uc + 1]),
                wait=(("T", t_u_last),), inc="S")
            pp_s[lane] = u_s

        uq = []
        for sp in range(NSP):
            srcw = [x_tok[sp][0]] + (bw if sp == 0 else [])
            P1["mid_waits"] = [x_tok[sp][1]]
            steps = ln_stats_steps(
                P1, 512,
                lambda dc, sp=sp: XBF[:, dc * 2048 + sp * 512:dc * 2048 + (sp + 1) * 512],
                t_src_waits=srcw, v_src_waits=[x_tok[sp][0]])
            for i, f in enumerate(steps):
                f()
                if i >= NDC - 1 and uq:
                    uq.pop(0)()
            for dc in range(NDC):
                vt_ = pb.emit("vector", lambda v, dc=dc, sp=sp: v.tensor_tensor(
                    SQ[:, 512:1024], XBF[:, dc * 2048 + sp * 512:dc * 2048 + (sp + 1) * 512],
                    ABC[:, 0:512], ALU.mult),
                    wait=(("V", pb.cnt("V")),), inc="V")
                xn_v = pb.emit("vector", lambda v, dc=dc, sp=sp: v.tensor_tensor(
                    AXv(dc, sp * 512, 512), SQ[:, 512:1024], ABC[:, 512:1024], ALU.add),
                    wait=(("V", vt_),), inc="V")
            xn_v_sp[sp] = xn_v
            pb.war_v = xn_v
            for uc in range(NDC):
                uq.append(lambda uc=uc, sp=sp: u_tile(uc, sp))
            uq.extend(kt_tile_steps(0, 0, sp, qkv_tok_g0,
                                    extra_w=[("V", xn_v_sp[sp])]))
            uq.extend(vt_tile_steps(0, 0, sp, qkv_tok_g0,
                                    extra_w=[("V", xn_v_sp[sp])]))
            while len(uq) > 30:
                uq.pop(0)()
        while uq:
            uq.pop(0)()
        if s == 0:
            for tn, buf in (("abc", ABC), ("ra", RA), ("rb", RB), ("sq", SQ)):
                if tn in taps:
                    pb.gdma(lambda sy, buf=buf, tn=tn: sy.dma_start(tap_t[tn], buf[:, :] if buf.shape[0] == 128 else buf[0:1, :]),
                            wait=(("V", xn_v), ("V", pb.cnt("V"))))
        if "xn" in taps and s == 0:
            pb.gdma(lambda sy: sy.dma_start(tap_t["xn"], AX[:, :]), wait=(("V", xn_v),))
        if upto == "p1":
            stop = True
        if stop:
            continue

        w2_tok = pb.gdma(lambda g: g.dma_start(
            WBIG[:, :].rearrange("p (c f) -> p c f", f=1024), w2_v),
            wait=(("T", t_u_last),))
        if "u" in taps and s == 0:
            pb.gdma(lambda sy: sy.dma_start(tap_t["u"], U[:, :]), wait=(("S", u_s),))
        if upto == "p2u":
            stop = True
        if stop:
            continue


        for hg in range(NHG):
            lw = hg % 2
            bf_ = hg % 2
            proj_q = []
            if hg + 1 < NHG:
                lw2 = (hg + 1) % 2
                w = [("T", wqkv_t[lw2])] if wqkv_t[lw2] else []
                ntok = pb.gdma(lambda g, hg=hg, lw2=lw2: g.dma_start(
                    WQKV[:, lw2 * 3072:(lw2 + 1) * 3072].rearrange("p (c f) -> p c f", f=384),
                    w1g_v[:, :, (hg + 1) * 384:(hg + 2) * 384]), wait=w)
                for sp in range(NSP):
                    proj_q.extend(kt_tile_steps(hg + 1, lw2, sp, ntok))
                for tb in range(4):
                    proj_q.extend(vt_tile_steps(hg + 1, lw2, tb, ntok))
            kt_s = kt_done[bf_]
            vt_s = vt_done[bf_]
            if "kt0" in taps and s == 0 and hg == 0:
                pb.gdma(lambda sy: sy.dma_start(tap_t["kt0"], KT[:, 0:2048]), wait=(("S", kt_s),))
            if "vt0" in taps and s == 0 and hg == 0:
                pb.gdma(lambda sy: sy.dma_start(tap_t["vt0"], VT[:, 0:2048]), wait=(("S", vt_s),))
            for sp in range(NSP):
                lq = pp2_lane()
                w = [("S", kt_s), ("S", vt_s)]
                if pp2_use[lq]:
                    w.append((pp2_eng[lq], pp2_use[lq]))
                if qsp_t:
                    w.append(("T", qsp_t))
                tl = 0
                for dc in range(NDC):
                    r = pb.emit("tensor", lambda t, sp=sp, dc=dc, lq=lq, lw=lw: t.matmul(
                        PP2[:, lq * 512:(lq + 1) * 512],
                        WQKV[:, lw * 3072 + dc * 384:lw * 3072 + dc * 384 + 128],
                        AXv(dc, sp * 512, 512),
                        start=(dc == 0), stop=(dc == NDC - 1)),
                        wait=w if dc == 0 else (), inc="T" if dc == NDC - 1 else None)
                    if r is not None:
                        tl = r
                q_s = pb.emit("scalar", lambda s2, hg=hg, lq=lq: s2.activation(
                    QSP[:, lq % 2 * 512:(lq % 2 + 1) * 512], PP2[:, lq * 512:(lq + 1) * 512],
                    AF.Silu, bias=B1Q[:, hg:hg + 1]),
                    wait=(("T", tl),), inc="S")
                pp2_use[lq] = q_s
                pp2_eng[lq] = "S"
                lqb = lq % 2
                la = av_i % 2
                av_i += 1

                def emit_av(kc, ls, la, bf_=bf_):
                    pb.emit("tensor", lambda t, kc=kc, ls=ls, la=la, bf_=bf_: t.matmul(
                        AVP[0:64, la * 512:(la + 1) * 512],
                        VT[:, bf_ * 2048 + kc * 128:bf_ * 2048 + kc * 128 + 64],
                        ATT[:, ls * 1024:ls * 1024 + 512],
                        start=(kc == 0), stop=(kc == NKC - 1), tile_position=(0, 0)),
                        wait=(("S", sp_evac_s[ls]),))
                    att_t[ls] = pb.emit("tensor", lambda t, kc=kc, ls=ls, la=la, bf_=bf_: t.matmul(
                        AVP[64:128, la * 512:(la + 1) * 512],
                        VT[:, bf_ * 2048 + kc * 128 + 64:bf_ * 2048 + (kc + 1) * 128],
                        ATT[:, ls * 1024 + 512:(ls + 1) * 1024],
                        start=(kc == 0), stop=(kc == NKC - 1), tile_position=(0, 64)),
                        inc="T")

                for kc in range(NKC):
                    ls = kc % 2
                    w = [("S", q_s)] if kc == 0 else []
                    if sp_evac_s[ls]:
                        w.append(("S", sp_evac_s[ls]))
                    if kc == 0 and avp_v[la]:
                        w.append(("V", avp_v[la]))
                    pb.emit("tensor", lambda t, kc=kc, ls=ls, lqb=lqb, bf_=bf_: t.matmul(
                        SP[:, ls * 1024:ls * 1024 + 512],
                        KT[0:64, bf_ * 2048 + kc * 128:bf_ * 2048 + (kc + 1) * 128],
                        QSP[0:64, lqb * 512:(lqb + 1) * 512],
                        start=True, stop=True, tile_position=(0, 0)), wait=w)
                    t_sc = pb.emit("tensor", lambda t, kc=kc, ls=ls, lqb=lqb, bf_=bf_: t.matmul(
                        SP[:, ls * 1024 + 512:ls * 1024 + 1024],
                        KT[64:128, bf_ * 2048 + kc * 128:bf_ * 2048 + (kc + 1) * 128],
                        QSP[64:128, lqb * 512:(lqb + 1) * 512],
                        start=True, stop=True, tile_position=(64, 0)), inc="T")
                    w2_ = [("T", t_sc)]
                    if att_t[ls]:
                        w2_.append(("T", att_t[ls]))
                    sp_evac_s[ls] = pb.emit("scalar", lambda s2, ls=ls: s2.activation(
                        ATT[:, ls * 1024:(ls + 1) * 1024], SP[:, ls * 1024:(ls + 1) * 1024],
                        AF.Silu), wait=w2_, inc="S")
                    if kc >= 1:
                        emit_av(kc - 1, (kc - 1) % 2, la)
                    if proj_q:
                        proj_q.pop(0)()
                    if proj_q and len(proj_q) > (NKC - kc) + 40:
                        proj_q.pop(0)()
                emit_av(NKC - 1, (NKC - 1) % 2, la)
                qsp_t = att_t[(NKC - 1) % 2]
                m_v = pb.emit("vector", lambda v, hg=hg, sp=sp, la=la: v.scalar_tensor_tensor(
                    U[:, hg * 2048 + sp * 512:hg * 2048 + (sp + 1) * 512],
                    AVP[:, la * 512:(la + 1) * 512], ISEQ,
                    U[:, hg * 2048 + sp * 512:hg * 2048 + (sp + 1) * 512],
                    ALU.mult, ALU.mult),
                    wait=(("T", qsp_t),), inc="V")
                avp_v[la] = m_v
            while proj_q:
                proj_q.pop(0)()
            wqkv_t[lw] = qsp_t
            ktvt_t[bf_] = qsp_t
        if "m" in taps and s == 0:
            pb.gdma(lambda sy: sy.dma_start(tap_t["m"], U[:, :]), wait=(("V", m_v),))
        if upto == "groups":
            stop = True
        if stop:
            continue

        pb.barrier()
        P4 = {
            "SX": TView(PSUMA, 0, 512),
            "SXX": TView(PSUMA, 512, 512),
            "ABCP": TView(PSUMA, 1024, 512),
            "sq_eng": "scalar",
            "abc_eng": "scalar",
            "no_war_v": True,
        }
        PW = TView(PSUMA, 2048, 2048)
        x1_v = 0
        pw_t = 0
        pw_dco = [[0] * NDC, [0] * NDC]
        an_s = [0, 0]
        an_used_t = [0, 0]
        tmp_hist = [0, 0, 0, 0]
        ntmp = [lambda: ABC[:, 512:768], lambda: ABC[:, 768:1024],
                lambda: SQ[:, 256:512], lambda: SQ[:, 768:1024]]

        def p4_stat_steps(sp8):
            lo = sp8 * 256
            return ln_stats_steps(
                P4, 256,
                lambda dc, lo=lo: U[:, dc * 2048 + lo:dc * 2048 + lo + 256],
                t_src_waits=[("V", m_v)] if sp8 == 0 else [],
                v_src_waits=[("V", m_v)] if sp8 == 0 else [])

        def p4_norm_steps(sp8):
            lo = sp8 * 256
            lane = sp8 % 2

            def mk_norm(dc):
                def go():
                    sl = dc % 4
                    w = [P4["abc_ready"]]
                    if tmp_hist[sl]:
                        w.append(("V", tmp_hist[sl]))
                    v1 = pb.emit("gpsimd", lambda g, dc=dc, lo=lo, sl=sl: g.tensor_tensor(
                        ntmp[sl](), U[:, dc * 2048 + lo:dc * 2048 + lo + 256],
                        ABC[:, 0:256], ALU.mult),
                        wait=w, inc="P")
                    v2 = pb.emit("vector", lambda v, sl=sl: v.tensor_tensor(
                        ntmp[sl](), ntmp[sl](), ABC[:, 256:512], ALU.add),
                        wait=(("P", v1),), inc="V")
                    w = [("V", v2)]
                    if dc == 0 and an_used_t[lane]:
                        w.append(("T", an_used_t[lane]))
                    an_s[lane] = pb.emit("vector", lambda v, dc=dc, lane=lane, sl=sl: v.tensor_scalar(
                        AN[:, lane * 2048 + dc * 256:lane * 2048 + (dc + 1) * 256], ntmp[sl](),
                        GAM[:, dc:dc + 1], BET[:, dc:dc + 1], ALU.mult, ALU.add),
                        wait=w, inc="V")
                    tmp_hist[sl] = an_s[lane]
                    P4["abc_readers"] = [("P", v1), ("V", v2)]
                    pb.war_v = an_s[lane]
                return go
            return [mk_norm(dc) for dc in range(NDC)]

        def p4_w2_chunks(sp8):
            lo = sp8 * 256
            lane = sp8 % 2

            def mk(dco):
                def go():
                    nonlocal pw_t
                    w = []
                    if dco == 0:
                        w = [("V", an_s[lane])]
                        if sp8 == 0:
                            w.append(w2_tok)
                        if x1_v:
                            w.append(("V", x1_v))
                    for dc in range(NDC):
                        r = pb.emit("tensor", lambda t, dco=dco, dc=dc, lane=lane: t.matmul(
                            PW[:, dco * 256:(dco + 1) * 256],
                            WBIG[:, dc * 1024 + dco * 128:dc * 1024 + (dco + 1) * 128],
                            AN[:, lane * 2048 + dc * 256:lane * 2048 + (dc + 1) * 256],
                            start=(dc == 0), stop=(dc == NDC - 1)),
                            wait=w if dc == 0 else (), inc="T" if dc == NDC - 1 else None)
                        if r is not None:
                            pw_t = r
                    pw_dco[lane][dco] = pw_t
                    if dco == NDC - 1:
                        an_used_t[lane] = pw_t
                return go

            def x1_step(dq):
                def go():
                    nonlocal x1_v
                    x1_v = pb.emit("vector", lambda v, dq=dq, lo=lo: v.scalar_tensor_tensor(
                        AXv(dq, lo, 256), PW[:, dq * 256:(dq + 1) * 256],
                        B2L[:, dq:dq + 1], XBF[:, dq * 2048 + lo:dq * 2048 + lo + 256],
                        ALU.add, ALU.add),
                        wait=(("T", pw_dco[lane][dq | 1]),), inc="V")
                return go
            return [mk(dco) for dco in range(NDC)] + [x1_step(dq) for dq in range(NDC)]

        w2q = []
        carry = []
        for f in p4_stat_steps(0):
            f()
        for sp8 in range(NSP8):
            nq = carry + (p4_stat_steps(sp8 + 1) if sp8 + 1 < NSP8 else [])
            keep = 1 if sp8 + 1 < NSP8 else 0
            for i, f in enumerate(p4_norm_steps(sp8)):
                for _ in range(2):
                    if w2q:
                        w2q.pop(0)()
                for _ in range(2 if i >= 4 else 1):
                    if len(nq) > keep:
                        nq.pop(0)()
                f()
            carry = nq
            w2q.extend(p4_w2_chunks(sp8))
        while w2q:
            w2q.pop(0)()
        if "x1" in taps and s == 0:
            pb.gdma(lambda sy: sy.dma_start(tap_t["x1"], AX[:, :]), wait=(("V", x1_v),))
        if "an" in taps and s == 0:
            pb.gdma(lambda sy: sy.dma_start(tap_t["an"], AN[:, 0:2048]), wait=(("V", max(an_s)),))
        if upto == "p4":
            stop = True
        if stop:
            continue

        fbar = pb.barrier()
        fbw = [("T", fbar[0]), ("V", fbar[1]), ("S", fbar[2])]
        _o = 0
        def _mk2(n):
            nonlocal _o
            v = TView(ARENA, _o, n)
            _o += n
            return v
        WG = _mk2(NDC * HID)
        WU = _mk2(NDC * HID)
        WD = _mk2(24 * 1024)
        XN2 = _mk2(4096)
        SGB = _mk2(512)
        HCH = _mk2(1024)

        wg_tok, wu_tok, wd_tok = [], [], []
        for hb in range(NHB):
            wg_tok.append(pb.gdma(lambda g, hb=hb: g.dma_start(
                WG[:, :].rearrange("p (c h) -> p c h", h=HID)[:, :, hb * 256:(hb + 1) * 256],
                wgt_v[:, :, hb * 256:(hb + 1) * 256]), wait=fbw if hb == 0 else ()))
            wu_tok.append(pb.gdma(lambda g, hb=hb: g.dma_start(
                WU[:, :].rearrange("p (c h) -> p c h", h=HID)[:, :, hb * 256:(hb + 1) * 256],
                wut_v[:, :, hb * 256:(hb + 1) * 256]), wait=fbw if hb == 0 else ()))
            wd_tok.append(pb.gdma(lambda g, hb=hb: g.dma_start(
                WD[:, :].rearrange("p (c d) -> p c d", d=1024)[:, hb * 2:(hb + 1) * 2, :],
                wdt_v[:, hb * 2:(hb + 1) * 2, :]), wait=fbw if hb == 0 else ()))

        PD = TView(PSUMA, 2048, 2048)
        xn2_v = [0, 0]
        xn2_used_t = [0, 0]
        out_v = 0
        sg_v = 0
        hch_t = 0
        pd_evac_v = 0
        F3 = {"SX": TView(PSUMA, 1536, 256),
              "SXX": TView(PSUMA, 1024, 256),
              "ABCP": TView(PSUMA, 1536, 512)}

        def ln3_steps(sp8):
            lo = sp8 * 256
            lane = sp8 % 2
            steps = ln_stats_steps(
                F3, 256,
                lambda dc, lo=lo: AXv(dc, lo, 256),
                t_src_waits=(fbw if sp8 == 0 else []),
                v_src_waits=(fbw if sp8 == 0 else []))

            def mk_norm(dc):
                def go():
                    w = [("V", pb.cnt("V"))]
                    if dc == 0 and xn2_used_t[lane]:
                        w.append(("T", xn2_used_t[lane]))
                    v1 = pb.emit("vector", lambda v, dc=dc, lo=lo: v.tensor_tensor(
                        SQ[:, 512:768], AXv(dc, lo, 256), ABC[:, 0:256], ALU.mult),
                        wait=w, inc="V")
                    xn2_v[lane] = pb.emit("vector", lambda v, dc=dc, lane=lane: v.tensor_tensor(
                        XN2[:, lane * 2048 + dc * 256:lane * 2048 + (dc + 1) * 256],
                        SQ[:, 512:768], ABC[:, 256:512], ALU.add),
                        wait=(("V", v1),), inc="V")
                    pb.war_v = xn2_v[lane]
                return go
            return steps + [mk_norm(dc) for dc in range(NDC)]

        for f in ln3_steps(0):
            f()
        for sp8 in range(NSP8):
            lo = sp8 * 256
            lane = sp8 % 2
            ln3_q = ln3_steps(sp8 + 1) if sp8 + 1 < NSP8 else []
            if "xn2" in taps and s == 0 and sp8 == 0:
                pb.gdma(lambda sy: sy.dma_start(tap_t["xn2"], XN2[:, 0:2048]), wait=(("V", xn2_v[0]),))
            GU = TView(PSUMA, 0, 1024)
            gu_s = [0, 0]
            gu_v = [0, 0]
            hch_used_t = [0, 0, 0, 0]
            pend_down = []

            def emit_down(hidc, lh, v_h):
                nonlocal hch_t, pd_evac_v
                w = [("V", v_h)]
                if hidc == 0 and pd_evac_v:
                    w.append(("V", pd_evac_v))
                for dco in range(NDC):
                    r = pb.emit("tensor", lambda t, hidc=hidc, dco=dco, lh=lh: t.matmul(
                        PD[:, dco * 256:(dco + 1) * 256],
                        WD[:, hidc * 1024 + dco * 128:hidc * 1024 + (dco + 1) * 128],
                        HCH[:, lh * 256:(lh + 1) * 256],
                        start=(hidc == 0 and dco % 2 == 0), stop=(hidc == 23),
                        skip_group_check=True),
                        wait=w if dco == 0 else (), inc="T" if dco == NDC - 1 else None)
                    if r is not None:
                        hch_t = r

            for hidc in range(24):
                lg = hidc % 2
                hb = hidc // 2
                w = []
                if hidc == 0:
                    w = [("V", xn2_v[lane]), wg_tok[hb], wu_tok[hb], wd_tok[hb]]
                elif hidc % 2 == 0:
                    w = [wg_tok[hb], wu_tok[hb], wd_tok[hb]]
                if gu_s[lg]:
                    w = w + [("S", gu_s[lg])]
                if gu_v[lg]:
                    w = w + [("V", gu_v[lg])]
                for dc in range(NDC):
                    pb.emit("tensor", lambda t, hidc=hidc, dc=dc, lg=lg, lane=lane: t.matmul(
                        GU[:, lg * 512:lg * 512 + 256],
                        WG[:, dc * HID + hidc * 128:dc * HID + (hidc + 1) * 128],
                        XN2[:, lane * 2048 + dc * 256:lane * 2048 + (dc + 1) * 256],
                        start=(dc == 0), stop=(dc == NDC - 1)),
                        wait=w if dc == 0 else ())
                t_gu = 0
                for dc in range(NDC):
                    r = pb.emit("tensor", lambda t, hidc=hidc, dc=dc, lg=lg, lane=lane: t.matmul(
                        GU[:, lg * 512 + 256:lg * 512 + 512],
                        WU[:, dc * HID + hidc * 128:dc * HID + (hidc + 1) * 128],
                        XN2[:, lane * 2048 + dc * 256:lane * 2048 + (dc + 1) * 256],
                        start=(dc == 0), stop=(dc == NDC - 1)),
                        inc="T" if dc == NDC - 1 else None)
                    if r is not None:
                        t_gu = r
                xn2_used_t[lane] = t_gu
                for _ in range(2):
                    if ln3_q and (hidc >= 2 or sp8 == 0):
                        ln3_q.pop(0)()
                wsg = [("T", t_gu)]
                if sg_v:
                    wsg.append(("V", sg_v))
                s_g = pb.emit("scalar", lambda s2, lg=lg: s2.activation(
                    SGB[:, lg * 256:(lg + 1) * 256],
                    GU[:, lg * 512:lg * 512 + 256], AF.Silu, scale=CQI[:, 0:1]),
                    wait=wsg, inc="S")
                gu_s[lg] = s_g
                lh = hidc % 4
                wh = [("S", s_g)]
                if hch_used_t[lh]:
                    wh.append(("T", hch_used_t[lh]))
                v_h = pb.emit("vector", lambda v, lg=lg, lh=lh: v.scalar_tensor_tensor(
                    HCH[:, lh * 256:(lh + 1) * 256],
                    SGB[:, lg * 256:(lg + 1) * 256], CQI[:, 0:1],
                    GU[:, lg * 512 + 256:lg * 512 + 512], ALU.mult, ALU.mult),
                    wait=wh, inc="V")
                gu_v[lg] = v_h
                sg_v = v_h
                pend_down.append((hidc, lh, v_h))
                if hidc >= 2:
                    h2, l2, v2 = pend_down.pop(0)
                    emit_down(h2, l2, v2)
                    hch_used_t[l2] = hch_t
            while pend_down:
                h2, l2, v2 = pend_down.pop(0)
                emit_down(h2, l2, v2)
                hch_used_t[l2] = hch_t
            while ln3_q:
                ln3_q.pop(0)()
            for dco in range(NDC):
                lq_ = dco % 4
                w = [("T", hch_t)] if dco == 0 else []
                if len(out_toks) >= 4:
                    w.append(out_toks[-4])
                out_v = pb.emit("vector", lambda v, dco=dco, lq_=lq_, lo=lo: v.scalar_tensor_tensor(
                    OUTB[:, lq_ * 256:(lq_ + 1) * 256], PD[:, dco * 256:(dco + 1) * 256],
                    CQI[:, 0:1], AXv(dco, lo, 256), ALU.mult, ALU.add),
                    wait=w, inc="V")
                pd_evac_v = out_v
                out_toks.append(pb.ydma(lambda sy, dco=dco, lq_=lq_, lo=lo, s=s: sy.dma_start(
                    yT[dco * 128:(dco + 1) * 128, s * SEQ + lo:s * SEQ + lo + 256],
                    OUTB[:, lq_ * 256:(lq_ + 1) * 256]),
                    wait=(("V", out_v),)))

    for n in ("y0", "y1", "g0", "g1", "g2", "g3"):
        if pb.cnt(n):
            pb.emit("sync", None, wait=((n, pb.cnt(n)),))

    with nc.Block() as block:
        @block.tensor
        def _(e):
            for f in pb.ops["tensor"]:
                f(e)

        @block.vector
        def _(e):
            for f in pb.ops["vector"]:
                f(e)

        @block.scalar
        def _(e):
            for f in pb.ops["scalar"]:
                f(e)

        @block.sync
        def _(e):
            for f in pb.ops["sync"]:
                f(e)

        @block.gpsimd
        def _(e):
            for f in pb.ops["gpsimd"]:
                f(e)

    return nc


def _host_layout(inputs):
    x = np.asarray(inputs["L0_attn_x"], dtype=np.float32)
    w1 = np.asarray(inputs["L0_attn_W1"], dtype=np.float32)
    b1 = np.asarray(inputs["L0_attn_b1"], dtype=np.float32)
    w2 = np.asarray(inputs["L0_attn_W2"], dtype=np.float32)
    b2 = np.asarray(inputs["L0_attn_b2"], dtype=np.float32)
    gam = np.asarray(inputs["L0_attn_gamma"], dtype=np.float32)
    bet = np.asarray(inputs["L0_attn_beta"], dtype=np.float32)
    wg = np.asarray(inputs["L0_ffn_w_gate"], dtype=np.int32)
    wu = np.asarray(inputs["L0_ffn_w_up"], dtype=np.int32)
    wd = np.asarray(inputs["L0_ffn_w_down"], dtype=np.int32)

    w1g = np.concatenate(
        [np.concatenate([w1[:, 2048 + g * 128:2048 + (g + 1) * 128],
                         w1[:, 3072 + g * 128:3072 + (g + 1) * 128],
                         w1[:, 1024 + g * 128:1024 + (g + 1) * 128]], axis=1)
         for g in range(NHG)], axis=1)
    shared = {
        "w1u": np.ascontiguousarray(w1[:, 0:1024]),
        "w1g": np.ascontiguousarray(w1g),
        "w2": np.ascontiguousarray(w2),
        "b1u": np.ascontiguousarray(b1[0:1024].reshape(8, 128).T),
        "b1q": np.ascontiguousarray(b1[2048:3072].reshape(8, 128).T),
        "b1k": np.ascontiguousarray(b1[3072:4096].reshape(8, 128).T),
        "b1vr": np.ascontiguousarray(np.broadcast_to(b1[1024:2048], (128, 1024))),
        "b2l": np.ascontiguousarray(b2.reshape(8, 128).T),
        "gaml": np.ascontiguousarray(gam.reshape(8, 128).T),
        "betl": np.ascontiguousarray(bet.reshape(8, 128).T),
        "wgt": np.ascontiguousarray(wg.T),
        "wut": np.ascontiguousarray(wu.T),
        "wdt": np.ascontiguousarray(wd.T),
    }
    in_maps = []
    for c in range(8):
        m = dict(shared)
        xc = x[2 * c:2 * c + 2]
        m["xT"] = np.ascontiguousarray(xc.transpose(2, 0, 1).reshape(D, NSEQ * SEQ))
        in_maps.append(m)
    return in_maps


def kernel(**inputs):
    in_maps = _host_layout(inputs)
    nc = build_program()
    res = run_bass_kernel_spmd(nc, in_maps, core_ids=list(range(8)))
    return np.concatenate(
        [np.asarray(res.results[c]["yT"]).reshape(D, NSEQ, SEQ).transpose(1, 2, 0)
         for c in range(8)], axis=0)


# version 27 (final)
# speedup vs baseline: 1.0624x; 1.0624x over previous
import numpy as np
from contextlib import ExitStack

import concourse.bass as bass
import concourse.mybir as mybir
from concourse.bass_utils import run_bass_kernel_spmd

F32 = mybir.dt.float32
BF16 = mybir.dt.bfloat16
I32 = mybir.dt.int32
AF = mybir.ActivationFunctionType
ALU = mybir.AluOpType

D = 1024
SEQ = 2048
NSEQ = 2
HID = 3072
NDC = 8
NHG = 8
NKC = 16
NSP = 4
NSP8 = 8
NHB = 12
EPS = 1e-5
QI = float(1.0 / 127.0)
ISEQ = float(1.0 / 2048.0)
ID = float(1.0 / 1024.0)

ENGINES = ("tensor", "vector", "scalar", "sync", "gpsimd")


class TView:

    def __init__(self, t, off, n):
        self.t = t
        self.off = off
        self.n = n

    def __getitem__(self, idx):
        p, c = idx
        a = 0 if c.start is None else c.start
        b = self.n if c.stop is None else c.stop
        return self.t[p, self.off + a:self.off + b]


class PB:

    def __init__(self, nc, es):
        self.nc = nc
        self.ops = {e: [] for e in ENGINES}
        self.sems = {}
        for n in ("T", "V", "S", "P", "g0", "g1", "g2", "g3", "y0", "y1",
                  "bT", "bV", "bS"):
            self.sems[n] = [es.enter_context(nc.semaphore(n)), 0]
        self.gj = 0
        self.yj = 0
        self.war_v = 0
        self.war_s = 0
        self.trace = []

    def cnt(self, n):
        return self.sems[n][1]

    def emit(self, eng, fn=None, wait=(), inc=None, incd=1):
        wl = [(self.sems[n][0], v) for (n, v) in wait if v > 0]
        post = None
        if inc is not None:
            self.sems[inc][1] += incd
            post = (self.sems[inc][0], incd)

        def run(e, wl=wl, fn=fn, post=post):
            for h, v in wl:
                e.wait_ge(h, v)
            ins = fn(e) if fn is not None else None
            if post is not None:
                if ins is None:
                    ins = e.nop()
                ins.then_inc(post[0], post[1])

        import traceback as _tb
        cal = _tb.extract_stack(limit=4)[0]
        self.trace.append((eng, [(n, v) for (n, v) in wait if v > 0],
                           (inc, incd) if inc else None,
                           "%s:%d" % (cal.filename.split("/")[-1], cal.lineno)))
        self.ops[eng].append(run)
        return self.cnt(inc) if inc is not None else None

    def simulate(self):
        idx = {e: 0 for e in ENGINES}
        sems = {n: 0 for n in self.sems}
        per = {e: [t for t in self.trace if t[0] == e] for e in ENGINES}
        progress = True
        while progress:
            progress = False
            for e in ENGINES:
                while idx[e] < len(per[e]):
                    eng, waits, inc, where = per[e][idx[e]]
                    if all(sems[n] >= v for n, v in waits):
                        if inc:
                            sems[inc[0]] += inc[1]
                        idx[e] += 1
                        progress = True
                    else:
                        break
        stuck = {e: (idx[e], len(per[e])) for e in ENGINES if idx[e] < len(per[e])}
        if stuck:
            print("DEADLOCK: stuck engines:", stuck)
            print("sem state:", sems)
            for e in stuck:
                eng, waits, inc, where = per[e][idx[e]]
                bad = [(n, v, sems[n]) for n, v in waits if sems[n] < v]
                print("  %s stuck at op %d (%s) waiting %s" % (e, idx[e], where, bad))
        else:
            print("simulate: plan completes; ops:", {e: len(per[e]) for e in ENGINES})
        return stuck

    def gdma(self, fn, wait=()):
        lane = "g%d" % (self.gj % 4)
        use = self.gj // 4
        self.gj += 1
        w = list(wait)
        if use >= 1:
            w.append((lane, 16 * use))
        v = self.emit("gpsimd", fn, wait=w, inc=lane, incd=16)
        return (lane, v)

    def ydma(self, fn, wait=()):
        lane = "y%d" % (self.yj % 2)
        use = self.yj // 2
        self.yj += 1
        w = list(wait)
        if use >= 1:
            w.append((lane, 16 * use))
        v = self.emit("sync", fn, wait=w, inc=lane, incd=16)
        return (lane, v)

    def barrier(self):
        t = self.emit("tensor", None, inc="bT")
        v = self.emit("vector", None, inc="bV")
        s = self.emit("scalar", None, inc="bS")
        self.emit("tensor", None, wait=(("bV", v), ("bS", s)))
        self.emit("vector", None, wait=(("bT", t), ("bS", s)))
        self.emit("scalar", None, wait=(("bT", t), ("bV", v)))
        return self.cnt("T"), self.cnt("V"), self.cnt("S")


def build_program(upto="all", taps=()):
    nc = bass.Bass("TRN2", target_bir_lowering=False)
    taps = set(taps)

    xT = nc.dram_tensor("xT", [D, NSEQ * SEQ], F32, kind="ExternalInput").ap()
    yT = nc.dram_tensor("yT", [D, NSEQ * SEQ], F32, kind="ExternalOutput").ap()
    w1u = nc.dram_tensor("w1u", [D, D], F32, kind="ExternalInput").ap()
    w1g = nc.dram_tensor("w1g", [D, NHG * 384], F32, kind="ExternalInput").ap()
    w2 = nc.dram_tensor("w2", [D, D], F32, kind="ExternalInput").ap()
    b1u = nc.dram_tensor("b1u", [128, 8], F32, kind="ExternalInput").ap()
    b1q = nc.dram_tensor("b1q", [128, 8], F32, kind="ExternalInput").ap()
    b1k = nc.dram_tensor("b1k", [128, 8], F32, kind="ExternalInput").ap()
    b1vr = nc.dram_tensor("b1vr", [128, D], F32, kind="ExternalInput").ap()
    b2l = nc.dram_tensor("b2l", [128, 8], F32, kind="ExternalInput").ap()
    gaml = nc.dram_tensor("gaml", [128, 8], F32, kind="ExternalInput").ap()
    betl = nc.dram_tensor("betl", [128, 8], F32, kind="ExternalInput").ap()
    wgt = nc.dram_tensor("wgt", [D, HID], I32, kind="ExternalInput").ap()
    wut = nc.dram_tensor("wut", [D, HID], I32, kind="ExternalInput").ap()
    wdt = nc.dram_tensor("wdt", [HID, D], I32, kind="ExternalInput").ap()
    tap_t = {}
    for t in taps:
        shp = {"xn": [128, 16384], "u": [128, 16384], "kt0": [128, 2048],
               "vt0": [128, 2048], "m": [128, 16384], "x1": [128, 16384],
               "xn2": [128, 2048], "an": [128, 2048],
               "abc": [128, 1024], "ra": [1, 512], "rb": [1, 512],
               "sq": [128, 1024]}[t]
        tap_t[t] = nc.dram_tensor("tap_" + t, shp, F32, kind="ExternalOutput").ap()

    es = ExitStack()
    pb = PB(nc, es)

    B1U = es.enter_context(nc.sbuf_tensor("B1U", [128, 8], F32))
    B1Q = es.enter_context(nc.sbuf_tensor("B1Q", [128, 8], F32))
    B1K = es.enter_context(nc.sbuf_tensor("B1K", [128, 8], F32))
    B1VR = es.enter_context(nc.sbuf_tensor("B1VR", [128, D], BF16))
    B2L = es.enter_context(nc.sbuf_tensor("B2L", [128, 8], F32))
    GAM = es.enter_context(nc.sbuf_tensor("GAM", [128, 8], F32))
    BET = es.enter_context(nc.sbuf_tensor("BET", [128, 8], F32))
    ONESC = es.enter_context(nc.sbuf_tensor("ONESC", [128, 1], BF16))
    CEPS = es.enter_context(nc.sbuf_tensor("CEPS", [128, 1], F32))
    CQI = es.enter_context(nc.sbuf_tensor("CQI", [128, 1], F32))
    ONESR = es.enter_context(nc.sbuf_tensor("ONESR", [1, 128], F32))
    AX = es.enter_context(nc.sbuf_tensor("AX", [128, 16384], BF16))
    R0 = es.enter_context(nc.sbuf_tensor("R0", [1, 512], F32))
    R1 = es.enter_context(nc.sbuf_tensor("R1", [1, 512], F32))
    RM = es.enter_context(nc.sbuf_tensor("RM", [1, 512], F32))
    RA = es.enter_context(nc.sbuf_tensor("RA", [1, 512], F32))
    RB = es.enter_context(nc.sbuf_tensor("RB", [1, 512], F32))
    SQ = es.enter_context(nc.sbuf_tensor("SQ", [128, 1024], BF16))
    ABC = es.enter_context(nc.sbuf_tensor("ABC", [128, 1024], BF16))
    OUTB = es.enter_context(nc.sbuf_tensor("OUTB", [128, 1024], F32))
    ARENA = es.enter_context(nc.sbuf_tensor("ARENA", [128, 79360], BF16))
    PSUMA = es.enter_context(nc.psum_tensor("PSUMA", [128, 4096], F32))

    pb.gdma(lambda g: g.dma_start(B1U[:, :], b1u))
    pb.gdma(lambda g: g.dma_start(B1Q[:, :], b1q))
    pb.gdma(lambda g: g.dma_start(B1K[:, :], b1k))
    pb.gdma(lambda g: g.dma_start(B1VR[:, :], b1vr))
    pb.gdma(lambda g: g.dma_start(B2L[:, :], b2l))
    pb.gdma(lambda g: g.dma_start(GAM[:, :], gaml))
    pb.gdma(lambda g: g.dma_start(BET[:, :], betl))
    consts_w = [(n, pb.cnt(n)) for n in ("g0", "g1", "g2", "g3")]
    pb.emit("vector", lambda v: v.memset(ONESC[:, :], 1.0), inc="V")
    _vc = pb.emit("vector", lambda v: v.memset(CEPS[:, :], 100000.0), inc="V")
    _vc = pb.emit("vector", lambda v: v.reciprocal(CEPS[:, :], CEPS[:, :]),
                  wait=(("V", _vc),), inc="V")
    _vc = pb.emit("vector", lambda v: v.memset(CQI[:, :], 127.0),
                  wait=(("V", _vc),), inc="V")
    _vc = pb.emit("vector", lambda v: v.reciprocal(CQI[:, :], CQI[:, :]),
                  wait=(("V", _vc),), inc="V")
    pb.emit("vector", lambda v: v.memset(ONESR[:, :], 1.0), inc="V")
    pb.emit("scalar", None, wait=consts_w)
    pb.emit("vector", None, wait=consts_w)
    pb.emit("tensor", None, wait=consts_w)

    xT_v = xT.rearrange("(c p) t -> p c t", p=128)
    w1u_v = w1u.rearrange("(c p) f -> p c f", p=128)
    w1g_v = w1g.rearrange("(c p) f -> p c f", p=128)
    w2_v = w2.rearrange("(c p) f -> p c f", p=128)
    wgt_v = wgt.rearrange("(c p) h -> p c h", p=128)
    wut_v = wut.rearrange("(c p) h -> p c h", p=128)
    wdt_v = wdt.rearrange("(c p) d -> p c d", p=128)

    AXv = lambda dc, lo, w: AX[:, dc * 2048 + lo:dc * 2048 + lo + w]

    def ln_stats(P, W, src, t_src_waits, v_src_waits):
        SX, SXX, ABCP = P["SX"], P["SXX"], P["ABCP"]
        sq_t = P.setdefault("sq_t", [0, 0])
        t_last = 0
        for dc in range(NDC):
            h = dc % 2
            w = list(v_src_waits) if dc == 0 else []
            if dc < 2 and pb.war_v:
                w.append(("V", pb.war_v))
            if sq_t[h]:
                w.append(("T", sq_t[h]))
            v_sq = pb.emit("vector", lambda vv, dc=dc, h=h: vv.tensor_tensor(
                SQ[:, h * 512:h * 512 + W], src(dc), src(dc), ALU.mult),
                wait=w, inc="V")
            w = list(t_src_waits) if dc == 0 else []
            if dc == 0 and P.get("sx_reuse_v", 0):
                w.append(("V", P["sx_reuse_v"]))
            pb.emit("tensor", lambda t, dc=dc: t.matmul(
                SX[0:1, 0:W], ONESC[:, :], src(dc),
                start=(dc == 0), stop=(dc == NDC - 1)), wait=w)
            t_last = pb.emit("tensor", lambda t, dc=dc, h=h: t.matmul(
                SXX[0:1, 0:W], ONESC[:, :], SQ[:, h * 512:h * 512 + W],
                start=(dc == 0 and not P.get("packed")), stop=(dc == NDC - 1)),
                wait=(("V", v_sq),), inc="T")
            sq_t[h] = t_last
        pb.emit("vector", lambda v: v.tensor_scalar_mul(R0[0:1, 0:W], SX[0:1, 0:W], -ID),
                wait=(("T", t_last), ("S", pb.war_s)))
        vr = pb.emit("vector", lambda v: v.tensor_scalar_mul(R1[0:1, 0:W], SXX[0:1, 0:W], ID), inc="V")
        vm = pb.emit("vector", lambda v: v.tensor_tensor(RM[0:1, 0:W], R0[0:1, 0:W], R0[0:1, 0:W], ALU.mult),
                     wait=(("V", vr),), inc="V")
        v_rows = pb.emit("vector", lambda v: v.tensor_tensor(
            R1[0:1, 0:W], R1[0:1, 0:W], RM[0:1, 0:W], ALU.subtract),
            wait=(("V", vm),), inc="V")
        sl = pb.emit("scalar", lambda s2: s2.activation(RM[0:1, 0:W], R1[0:1, 0:W], AF.Ln, bias=CEPS[0:1, 0:1]),
                     wait=(("V", v_rows),), inc="S")
        s_rows = pb.emit("scalar", lambda s2: s2.activation(RA[0:1, 0:W], RM[0:1, 0:W], AF.Exp, scale=-0.5),
                         wait=(("S", sl),), inc="S")
        pb.war_s = s_rows
        v_rb = pb.emit("vector", lambda v: v.tensor_tensor(
            RB[0:1, 0:W], R0[0:1, 0:W], RA[0:1, 0:W], ALU.mult),
            wait=(("S", s_rows),), inc="V")
        w = [("V", v_rb)]
        if P.get("abc_reuse_v", 0):
            w.append(("V", P["abc_reuse_v"]))
        pb.emit("tensor", lambda t: t.matmul(ABCP[:, 0:W], ONESR[:, :], RA[0:1, 0:W],
                                             start=True, stop=True), wait=w)
        t_bc = pb.emit("tensor", lambda t: t.matmul(ABCP[:, W:2 * W], ONESR[:, :], RB[0:1, 0:W],
                                                    start=True, stop=True), inc="T")
        v_abc = pb.emit("vector", lambda v: v.tensor_copy(ABC[:, 0:2 * W], ABCP[:, 0:2 * W]),
                        wait=(("T", t_bc),), inc="V")
        P["abc_reuse_v"] = v_abc
        P["sx_reuse_v"] = v_abc
        return v_abc, t_bc

    def ln_stats_steps(P, W, src, t_src_waits, v_src_waits):
        SX, SXX, ABCP = P["SX"], P["SXX"], P["ABCP"]
        sq_eng = P.get("sq_eng", "vector")
        abc_eng = P.get("abc_eng", "vector")
        st = {"t_last": 0}

        def chunk(dc):
            def go():
                sq_t = P.setdefault("sq_t", [0, 0])
                h = dc % 2
                w = list(v_src_waits) if dc == 0 else []
                if dc == 4:
                    w.extend(P.get("mid_waits", ()))
                if dc < 2 and pb.war_v and not P.get("no_war_v"):
                    w.append(("V", pb.war_v))
                if sq_t[h]:
                    w.append(("T", sq_t[h]))
                if sq_eng == "scalar":
                    e_sq = ("S", pb.emit("scalar", lambda s2, dc=dc, h=h: s2.square(
                        SQ[:, h * 512:h * 512 + W], src(dc)),
                        wait=w, inc="S"))
                else:
                    e_sq = ("V", pb.emit("vector", lambda vv, dc=dc, h=h: vv.tensor_tensor(
                        SQ[:, h * 512:h * 512 + W], src(dc), src(dc), ALU.mult),
                        wait=w, inc="V"))
                w = list(t_src_waits) if dc == 0 else []
                if dc == 4:
                    w.extend(P.get("mid_waits", ()))
                if dc == 0 and P.get("sx_reuse", None):
                    w.append(P["sx_reuse"])
                pb.emit("tensor", lambda t, dc=dc: t.matmul(
                    SX[0:1, 0:W], ONESC[:, :], src(dc),
                    start=(dc == 0), stop=(dc == NDC - 1)), wait=w)
                st["t_last"] = pb.emit("tensor", lambda t, dc=dc, h=h: t.matmul(
                    SXX[0:1, 0:W], ONESC[:, :], SQ[:, h * 512:h * 512 + W],
                    start=(dc == 0 and not P.get("packed")), stop=(dc == NDC - 1)),
                    wait=(e_sq,), inc="T")
                P["sq_t"][h] = st["t_last"]
            return go

        def rows1():
            pb.emit("vector", lambda v: v.tensor_scalar_mul(R0[0:1, 0:W], SX[0:1, 0:W], -ID),
                    wait=(("T", st["t_last"]), ("S", pb.war_s)))
            vr = pb.emit("vector", lambda v: v.tensor_scalar_mul(R1[0:1, 0:W], SXX[0:1, 0:W], ID), inc="V")
            vm = pb.emit("vector", lambda v: v.tensor_tensor(RM[0:1, 0:W], R0[0:1, 0:W], R0[0:1, 0:W], ALU.mult),
                         wait=(("V", vr),), inc="V")
            st["v_rows"] = pb.emit("vector", lambda v: v.tensor_tensor(
                R1[0:1, 0:W], R1[0:1, 0:W], RM[0:1, 0:W], ALU.subtract),
                wait=(("V", vm),), inc="V")

        def rows2():
            sl = pb.emit("scalar", lambda s2: s2.activation(RM[0:1, 0:W], R1[0:1, 0:W], AF.Ln, bias=CEPS[0:1, 0:1]),
                         wait=(("V", st["v_rows"]),), inc="S")
            s_rows = pb.emit("scalar", lambda s2: s2.activation(RA[0:1, 0:W], RM[0:1, 0:W], AF.Exp, scale=-0.5),
                             wait=(("S", sl),), inc="S")
            pb.war_s = s_rows
            st["v_rb"] = pb.emit("vector", lambda v: v.tensor_tensor(
                RB[0:1, 0:W], R0[0:1, 0:W], RA[0:1, 0:W], ALU.mult),
                wait=(("S", s_rows),), inc="V")

        def bcast():
            w = [("V", st["v_rb"])]
            if P.get("abc_reuse", None):
                w.append(P["abc_reuse"])
            pb.emit("tensor", lambda t: t.matmul(ABCP[:, 0:W], ONESR[:, :], RA[0:1, 0:W],
                                                 start=True, stop=True), wait=w)
            t_bc = pb.emit("tensor", lambda t: t.matmul(ABCP[:, W:2 * W], ONESR[:, :], RB[0:1, 0:W],
                                                        start=True, stop=True), inc="T")
            wc = [("T", t_bc)]
            wc.extend(P.get("abc_readers", ()))
            if abc_eng == "scalar":
                e_abc = ("S", pb.emit("scalar", lambda s2: s2.copy(
                    ABC[:, 0:2 * W], ABCP[:, 0:2 * W]), wait=wc, inc="S"))
            else:
                e_abc = ("V", pb.emit("vector", lambda v: v.tensor_copy(
                    ABC[:, 0:2 * W], ABCP[:, 0:2 * W]), wait=wc, inc="V"))
            P["abc_reuse"] = e_abc
            P["sx_reuse"] = e_abc
            P["abc_ready"] = e_abc
        return [chunk(dc) for dc in range(NDC)] + [rows1, rows2, bcast]

    out_toks = []
    stop = False
    for s in range(NSEQ):
        if stop:
            break
        _o = 0
        def _mk(n):
            nonlocal _o
            v = TView(ARENA, _o, n)
            _o += n
            return v
        XBF = _mk(16384)
        U = _mk(16384)
        WBIG = _mk(8192)
        KT = _mk(4096)
        VT = _mk(4096)
        WQKV = _mk(6144)
        QSP = _mk(1024)
        ATT = _mk(2048)
        AN = _mk(4096)
        VTMP = _mk(512)

        bar = pb.barrier()
        bw = [("T", bar[0]), ("V", bar[1]), ("S", bar[2])]
        bwx = bw if s > 0 else []

        def x_half(sp, hh, w):
            return pb.gdma(lambda g, sp=sp, hh=hh, s=s: g.dma_start(
                XBF[:, :].rearrange("p (c t) -> p c t", c=NDC)
                [:, hh * 4:hh * 4 + 4, sp * 512:(sp + 1) * 512],
                xT_v[:, hh * 4:hh * 4 + 4, s * SEQ + sp * 512:s * SEQ + (sp + 1) * 512]),
                wait=w)
        x_tok = [[x_half(0, 0, bwx), x_half(0, 1, bwx)]]
        w1u_tok = pb.gdma(lambda g: g.dma_start(
            WBIG[:, :].rearrange("p (c f) -> p c f", f=1024), w1u_v), wait=bwx)
        qkv_tok_g0 = pb.gdma(lambda g: g.dma_start(
            WQKV[:, 0:3072].rearrange("p (c f) -> p c f", f=384),
            w1g_v[:, :, 0:384]))
        for sp in range(1, NSP):
            x_tok.append([x_half(sp, 0, ()), x_half(sp, 1, ())])

        if upto == "load":
            pb.gdma(lambda sy: sy.dma_start(tap_t["xn"], XBF[:, :]),
                    wait=[t for pr in x_tok for t in pr])
            stop = True
        if stop:
            continue
        SP = TView(PSUMA, 0, 2048)
        AVP = TView(PSUMA, 2048, 1024)
        PP2 = TView(PSUMA, 3072, 1024)
        sp_evac_s = [0, 0]
        att_t = [0, 0]
        avp_v = [0, 0]
        pp2_use = [0, 0]
        pp2_eng = ["S", "S"]
        pp2_i = [0]
        wqkv_t = [0, 0]
        ktvt_t = [0, 0]
        kt_done = [0, 0]
        vt_done = [0, 0]
        qsp_t = 0
        m_v = 0
        av_i = 0

        def pp2_lane():
            lane = pp2_i[0] % 2
            pp2_i[0] += 1
            return lane

        def kt_tile_steps(hg, lw, sp, qkv_tok, extra_w=()):
            bf_ = hg % 2
            lane = pp2_lane()
            st = {"tl": 0}

            def mk_mm(dc):
                def go():
                    w = []
                    if dc == 0:
                        w = [qkv_tok] + list(extra_w)
                        if pp2_use[lane]:
                            w.append((pp2_eng[lane], pp2_use[lane]))
                        if ktvt_t[bf_]:
                            w.append(("T", ktvt_t[bf_]))
                    r = pb.emit("tensor", lambda t, sp=sp, dc=dc, lane=lane, lw=lw: t.matmul(
                        PP2[:, lane * 512:(lane + 1) * 512],
                        WQKV[:, lw * 3072 + dc * 384 + 128:lw * 3072 + dc * 384 + 256],
                        AXv(dc, sp * 512, 512),
                        start=(dc == 0), stop=(dc == NDC - 1)),
                        wait=w, inc="T" if dc == NDC - 1 else None)
                    if r is not None:
                        st["tl"] = r
                return go

            def evac():
                kt_done[bf_] = pb.emit("scalar", lambda s2, hg=hg, sp=sp, lane=lane, bf_=bf_: s2.activation(
                    KT[:, bf_ * 2048 + sp * 512:bf_ * 2048 + (sp + 1) * 512],
                    PP2[:, lane * 512:(lane + 1) * 512],
                    AF.Silu, bias=B1K[:, hg:hg + 1]),
                    wait=(("T", st["tl"]),), inc="S")
                pp2_use[lane] = kt_done[bf_]
                pp2_eng[lane] = "S"
            steps = [mk_mm(dc) for dc in range(NDC)]
            steps[-1] = (lambda last=steps[-1]: (last(), evac()))
            return steps

        def vt_tile_steps(hg, lw, tb, qkv_tok, extra_w=()):
            bf_ = hg % 2
            lane = pp2_lane()
            st = {"tl": 0}

            def mk_mm(tci, dc):
                tc = tb * 4 + tci

                def go():
                    w = []
                    if tci == 0 and dc == 0:
                        w = [qkv_tok] + list(extra_w)
                        if pp2_use[lane]:
                            w.append((pp2_eng[lane], pp2_use[lane]))
                        if ktvt_t[bf_]:
                            w.append(("T", ktvt_t[bf_]))
                    r = pb.emit("tensor", lambda t, tc=tc, tci=tci, dc=dc, lane=lane, lw=lw: t.matmul(
                        PP2[:, lane * 512 + tci * 128:lane * 512 + (tci + 1) * 128],
                        AXv(dc, tc * 128, 128),
                        WQKV[:, lw * 3072 + dc * 384 + 256:lw * 3072 + dc * 384 + 384],
                        start=(dc == 0), stop=(dc == NDC - 1)),
                        wait=w, inc="T" if (tci == 3 and dc == NDC - 1) else None)
                    if r is not None:
                        st["tl"] = r
                return go

            def evac():
                b1s = B1VR[:, hg * 128:(hg + 1) * 128]
                rep = bass.AP(b1s.tensor, b1s.offset, [b1s.ap[0], [0, 4], [1, 128]])
                wv = [("T", st["tl"])]
                if pb.war_v:
                    wv.append(("V", pb.war_v))
                if vt_done[0] or vt_done[1]:
                    wv.append(("S", max(vt_done)))
                vv = pb.emit("vector", lambda v, lane=lane, rep=rep: v.tensor_tensor(
                    VTMP[:, 0:512], PP2[:, lane * 512:(lane + 1) * 512], rep, ALU.add),
                    wait=wv, inc="V")
                vt_done[bf_] = pb.emit("scalar", lambda s2, tb=tb, bf_=bf_: s2.activation(
                    VT[:, bf_ * 2048 + tb * 512:bf_ * 2048 + (tb + 1) * 512],
                    VTMP[:, 0:512], AF.Silu),
                    wait=(("V", vv),), inc="S")
                pp2_use[lane] = vv
                pp2_eng[lane] = "V"
            steps = []
            for tci in range(4):
                for dc in range(0, NDC, 2):
                    steps.append((lambda a=mk_mm(tci, dc), b=mk_mm(tci, dc + 1): (a(), b())))
            steps[-1] = (lambda last=steps[-1]: (last(), evac()))
            return steps


        P1 = {
            "SX": TView(PSUMA, 0, 512),
            "SXX": TView(PSUMA, 512, 512),
            "ABCP": TView(PSUMA, 1024, 1024),
        }
        PPs = (TView(PSUMA, 2048, 512), TView(PSUMA, 2560, 512))
        pp_s = [0, 0]
        pp_i = [0]
        u_s = 0
        t_u_last = 0
        xn_v = 0
        xn_v_sp = [0] * NSP

        def u_tile(uc, sp):
            nonlocal u_s, t_u_last
            lane = pp_i[0] % 2
            pp_i[0] += 1
            w = [("V", xn_v_sp[sp])]
            if uc == 0 and sp == 0:
                w.append(w1u_tok)
            if pp_s[lane]:
                w.append(("S", pp_s[lane]))
            for dc in range(NDC):
                r = pb.emit("tensor", lambda t, uc=uc, sp=sp, dc=dc, lane=lane: t.matmul(
                    PPs[lane][:, :], WBIG[:, dc * 1024 + uc * 128:dc * 1024 + (uc + 1) * 128],
                    AXv(dc, sp * 512, 512),
                    start=(dc == 0), stop=(dc == NDC - 1)),
                    wait=w if dc == 0 else (), inc="T" if dc == NDC - 1 else None)
                if r is not None:
                    t_u_last = r
            u_s = pb.emit("scalar", lambda s2, uc=uc, sp=sp, lane=lane: s2.activation(
                U[:, uc * 2048 + sp * 512:uc * 2048 + (sp + 1) * 512], PPs[lane][:, :],
                AF.Silu, bias=B1U[:, uc:uc + 1]),
                wait=(("T", t_u_last),), inc="S")
            pp_s[lane] = u_s

        uq = []
        for sp in range(NSP):
            srcw = [x_tok[sp][0]] + (bw if sp == 0 else [])
            P1["mid_waits"] = [x_tok[sp][1]]
            steps = ln_stats_steps(
                P1, 512,
                lambda dc, sp=sp: XBF[:, dc * 2048 + sp * 512:dc * 2048 + (sp + 1) * 512],
                t_src_waits=srcw, v_src_waits=[x_tok[sp][0]])
            for i, f in enumerate(steps):
                f()
                if i >= NDC - 1 and uq:
                    uq.pop(0)()
            for dc in range(NDC):
                vt_ = pb.emit("vector", lambda v, dc=dc, sp=sp: v.tensor_tensor(
                    SQ[:, 512:1024], XBF[:, dc * 2048 + sp * 512:dc * 2048 + (sp + 1) * 512],
                    ABC[:, 0:512], ALU.mult),
                    wait=(("V", pb.cnt("V")),), inc="V")
                xn_v = pb.emit("vector", lambda v, dc=dc, sp=sp: v.tensor_tensor(
                    AXv(dc, sp * 512, 512), SQ[:, 512:1024], ABC[:, 512:1024], ALU.add),
                    wait=(("V", vt_),), inc="V")
            xn_v_sp[sp] = xn_v
            pb.war_v = xn_v
            for uc in range(NDC):
                uq.append(lambda uc=uc, sp=sp: u_tile(uc, sp))
            uq.extend(kt_tile_steps(0, 0, sp, qkv_tok_g0,
                                    extra_w=[("V", xn_v_sp[sp])]))
            uq.extend(vt_tile_steps(0, 0, sp, qkv_tok_g0,
                                    extra_w=[("V", xn_v_sp[sp])]))
            while len(uq) > 30:
                uq.pop(0)()
        while uq:
            uq.pop(0)()
        if s == 0:
            for tn, buf in (("abc", ABC), ("ra", RA), ("rb", RB), ("sq", SQ)):
                if tn in taps:
                    pb.gdma(lambda sy, buf=buf, tn=tn: sy.dma_start(tap_t[tn], buf[:, :] if buf.shape[0] == 128 else buf[0:1, :]),
                            wait=(("V", xn_v), ("V", pb.cnt("V"))))
        if "xn" in taps and s == 0:
            pb.gdma(lambda sy: sy.dma_start(tap_t["xn"], AX[:, :]), wait=(("V", xn_v),))
        if upto == "p1":
            stop = True
        if stop:
            continue

        w2_tok = pb.gdma(lambda g: g.dma_start(
            WBIG[:, :].rearrange("p (c f) -> p c f", f=1024), w2_v),
            wait=(("T", t_u_last),))
        if "u" in taps and s == 0:
            pb.gdma(lambda sy: sy.dma_start(tap_t["u"], U[:, :]), wait=(("S", u_s),))
        if upto == "p2u":
            stop = True
        if stop:
            continue


        for hg in range(NHG):
            lw = hg % 2
            bf_ = hg % 2
            proj_q = []
            if hg + 1 < NHG:
                lw2 = (hg + 1) % 2
                w = [("T", wqkv_t[lw2])] if wqkv_t[lw2] else []
                ntok = pb.gdma(lambda g, hg=hg, lw2=lw2: g.dma_start(
                    WQKV[:, lw2 * 3072:(lw2 + 1) * 3072].rearrange("p (c f) -> p c f", f=384),
                    w1g_v[:, :, (hg + 1) * 384:(hg + 2) * 384]), wait=w)
                for sp in range(NSP):
                    proj_q.extend(kt_tile_steps(hg + 1, lw2, sp, ntok))
                for tb in range(4):
                    proj_q.extend(vt_tile_steps(hg + 1, lw2, tb, ntok))
            kt_s = kt_done[bf_]
            vt_s = vt_done[bf_]
            if "kt0" in taps and s == 0 and hg == 0:
                pb.gdma(lambda sy: sy.dma_start(tap_t["kt0"], KT[:, 0:2048]), wait=(("S", kt_s),))
            if "vt0" in taps and s == 0 and hg == 0:
                pb.gdma(lambda sy: sy.dma_start(tap_t["vt0"], VT[:, 0:2048]), wait=(("S", vt_s),))
            for sp in range(NSP):
                lq = pp2_lane()
                w = [("S", kt_s), ("S", vt_s)]
                if pp2_use[lq]:
                    w.append((pp2_eng[lq], pp2_use[lq]))
                if qsp_t:
                    w.append(("T", qsp_t))
                tl = 0
                for dc in range(NDC):
                    r = pb.emit("tensor", lambda t, sp=sp, dc=dc, lq=lq, lw=lw: t.matmul(
                        PP2[:, lq * 512:(lq + 1) * 512],
                        WQKV[:, lw * 3072 + dc * 384:lw * 3072 + dc * 384 + 128],
                        AXv(dc, sp * 512, 512),
                        start=(dc == 0), stop=(dc == NDC - 1)),
                        wait=w if dc == 0 else (), inc="T" if dc == NDC - 1 else None)
                    if r is not None:
                        tl = r
                q_s = pb.emit("scalar", lambda s2, hg=hg, lq=lq: s2.activation(
                    QSP[:, lq % 2 * 512:(lq % 2 + 1) * 512], PP2[:, lq * 512:(lq + 1) * 512],
                    AF.Silu, bias=B1Q[:, hg:hg + 1]),
                    wait=(("T", tl),), inc="S")
                pp2_use[lq] = q_s
                pp2_eng[lq] = "S"
                lqb = lq % 2
                la = av_i % 2
                av_i += 1

                def emit_av(kc, ls, la, bf_=bf_):
                    pb.emit("tensor", lambda t, kc=kc, ls=ls, la=la, bf_=bf_: t.matmul(
                        AVP[0:64, la * 512:(la + 1) * 512],
                        VT[:, bf_ * 2048 + kc * 128:bf_ * 2048 + kc * 128 + 64],
                        ATT[:, ls * 1024:ls * 1024 + 512],
                        start=(kc == 0), stop=(kc == NKC - 1), tile_position=(0, 0)),
                        wait=(("S", sp_evac_s[ls]),))
                    att_t[ls] = pb.emit("tensor", lambda t, kc=kc, ls=ls, la=la, bf_=bf_: t.matmul(
                        AVP[64:128, la * 512:(la + 1) * 512],
                        VT[:, bf_ * 2048 + kc * 128 + 64:bf_ * 2048 + (kc + 1) * 128],
                        ATT[:, ls * 1024 + 512:(ls + 1) * 1024],
                        start=(kc == 0), stop=(kc == NKC - 1), tile_position=(0, 64)),
                        inc="T")

                for kc in range(NKC):
                    ls = kc % 2
                    w = [("S", q_s)] if kc == 0 else []
                    if sp_evac_s[ls]:
                        w.append(("S", sp_evac_s[ls]))
                    if kc == 0 and avp_v[la]:
                        w.append(("V", avp_v[la]))
                    pb.emit("tensor", lambda t, kc=kc, ls=ls, lqb=lqb, bf_=bf_: t.matmul(
                        SP[:, ls * 1024:ls * 1024 + 512],
                        KT[0:64, bf_ * 2048 + kc * 128:bf_ * 2048 + (kc + 1) * 128],
                        QSP[0:64, lqb * 512:(lqb + 1) * 512],
                        start=True, stop=True, tile_position=(0, 0)), wait=w)
                    t_sc = pb.emit("tensor", lambda t, kc=kc, ls=ls, lqb=lqb, bf_=bf_: t.matmul(
                        SP[:, ls * 1024 + 512:ls * 1024 + 1024],
                        KT[64:128, bf_ * 2048 + kc * 128:bf_ * 2048 + (kc + 1) * 128],
                        QSP[64:128, lqb * 512:(lqb + 1) * 512],
                        start=True, stop=True, tile_position=(64, 0)), inc="T")
                    w2_ = [("T", t_sc)]
                    if att_t[ls]:
                        w2_.append(("T", att_t[ls]))
                    sp_evac_s[ls] = pb.emit("scalar", lambda s2, ls=ls: s2.activation(
                        ATT[:, ls * 1024:(ls + 1) * 1024], SP[:, ls * 1024:(ls + 1) * 1024],
                        AF.Silu), wait=w2_, inc="S")
                    if kc >= 1:
                        emit_av(kc - 1, (kc - 1) % 2, la)
                    if proj_q:
                        proj_q.pop(0)()
                    if proj_q and len(proj_q) > (NKC - kc) + 20:
                        proj_q.pop(0)()
                emit_av(NKC - 1, (NKC - 1) % 2, la)
                qsp_t = att_t[(NKC - 1) % 2]
                m_v = pb.emit("vector", lambda v, hg=hg, sp=sp, la=la: v.scalar_tensor_tensor(
                    U[:, hg * 2048 + sp * 512:hg * 2048 + (sp + 1) * 512],
                    AVP[:, la * 512:(la + 1) * 512], ISEQ,
                    U[:, hg * 2048 + sp * 512:hg * 2048 + (sp + 1) * 512],
                    ALU.mult, ALU.mult),
                    wait=(("T", qsp_t),), inc="V")
                avp_v[la] = m_v
            while proj_q:
                proj_q.pop(0)()
            wqkv_t[lw] = qsp_t
            ktvt_t[bf_] = qsp_t
        if "m" in taps and s == 0:
            pb.gdma(lambda sy: sy.dma_start(tap_t["m"], U[:, :]), wait=(("V", m_v),))
        if upto == "groups":
            stop = True
        if stop:
            continue

        pb.barrier()
        P4 = {
            "SX": TView(PSUMA, 0, 512),
            "SXX": TView(PSUMA, 512, 512),
            "ABCP": TView(PSUMA, 1024, 512),
            "sq_eng": "scalar",
            "abc_eng": "scalar",
            "no_war_v": True,
        }
        PW = TView(PSUMA, 2048, 2048)
        x1_v = 0
        pw_t = 0
        pw_dco = [[0] * NDC, [0] * NDC]
        an_s = [0, 0]
        an_used_t = [0, 0]
        tmp_hist = [0, 0, 0, 0]
        ntmp = [lambda: ABC[:, 512:768], lambda: ABC[:, 768:1024],
                lambda: SQ[:, 256:512], lambda: SQ[:, 768:1024]]

        def p4_stat_steps(sp8):
            lo = sp8 * 256
            return ln_stats_steps(
                P4, 256,
                lambda dc, lo=lo: U[:, dc * 2048 + lo:dc * 2048 + lo + 256],
                t_src_waits=[("V", m_v)] if sp8 == 0 else [],
                v_src_waits=[("V", m_v)] if sp8 == 0 else [])

        def p4_norm_steps(sp8):
            lo = sp8 * 256
            lane = sp8 % 2

            def mk_norm(dc):
                def go():
                    sl = dc % 4
                    w = [P4["abc_ready"]]
                    if tmp_hist[sl]:
                        w.append(("V", tmp_hist[sl]))
                    v1 = pb.emit("gpsimd", lambda g, dc=dc, lo=lo, sl=sl: g.tensor_tensor(
                        ntmp[sl](), U[:, dc * 2048 + lo:dc * 2048 + lo + 256],
                        ABC[:, 0:256], ALU.mult),
                        wait=w, inc="P")
                    v2 = pb.emit("vector", lambda v, sl=sl: v.tensor_tensor(
                        ntmp[sl](), ntmp[sl](), ABC[:, 256:512], ALU.add),
                        wait=(("P", v1),), inc="V")
                    w = [("V", v2)]
                    if dc == 0 and an_used_t[lane]:
                        w.append(("T", an_used_t[lane]))
                    an_s[lane] = pb.emit("vector", lambda v, dc=dc, lane=lane, sl=sl: v.tensor_scalar(
                        AN[:, lane * 2048 + dc * 256:lane * 2048 + (dc + 1) * 256], ntmp[sl](),
                        GAM[:, dc:dc + 1], BET[:, dc:dc + 1], ALU.mult, ALU.add),
                        wait=w, inc="V")
                    tmp_hist[sl] = an_s[lane]
                    P4["abc_readers"] = [("P", v1), ("V", v2)]
                    pb.war_v = an_s[lane]
                return go
            return [mk_norm(dc) for dc in range(NDC)]

        def p4_w2_chunks(sp8):
            lo = sp8 * 256
            lane = sp8 % 2

            def mk(dco):
                def go():
                    nonlocal pw_t
                    w = []
                    if dco == 0:
                        w = [("V", an_s[lane])]
                        if sp8 == 0:
                            w.append(w2_tok)
                        if x1_v:
                            w.append(("V", x1_v))
                    for dc in range(NDC):
                        r = pb.emit("tensor", lambda t, dco=dco, dc=dc, lane=lane: t.matmul(
                            PW[:, dco * 256:(dco + 1) * 256],
                            WBIG[:, dc * 1024 + dco * 128:dc * 1024 + (dco + 1) * 128],
                            AN[:, lane * 2048 + dc * 256:lane * 2048 + (dc + 1) * 256],
                            start=(dc == 0), stop=(dc == NDC - 1)),
                            wait=w if dc == 0 else (), inc="T" if dc == NDC - 1 else None)
                        if r is not None:
                            pw_t = r
                    pw_dco[lane][dco] = pw_t
                    if dco == NDC - 1:
                        an_used_t[lane] = pw_t
                return go

            def x1_step(dq):
                def go():
                    nonlocal x1_v
                    x1_v = pb.emit("vector", lambda v, dq=dq, lo=lo: v.scalar_tensor_tensor(
                        AXv(dq, lo, 256), PW[:, dq * 256:(dq + 1) * 256],
                        B2L[:, dq:dq + 1], XBF[:, dq * 2048 + lo:dq * 2048 + lo + 256],
                        ALU.add, ALU.add),
                        wait=(("T", pw_dco[lane][dq | 1]),), inc="V")
                return go
            return [mk(dco) for dco in range(NDC)] + [x1_step(dq) for dq in range(NDC)]

        w2q = []
        carry = []
        for f in p4_stat_steps(0):
            f()
        for sp8 in range(NSP8):
            nq = carry + (p4_stat_steps(sp8 + 1) if sp8 + 1 < NSP8 else [])
            keep = 1 if sp8 + 1 < NSP8 else 0
            for i, f in enumerate(p4_norm_steps(sp8)):
                for _ in range(2):
                    if w2q:
                        w2q.pop(0)()
                for _ in range(2 if i >= 4 else 1):
                    if len(nq) > keep:
                        nq.pop(0)()
                f()
            carry = nq
            w2q.extend(p4_w2_chunks(sp8))
        while w2q:
            w2q.pop(0)()
        if "x1" in taps and s == 0:
            pb.gdma(lambda sy: sy.dma_start(tap_t["x1"], AX[:, :]), wait=(("V", x1_v),))
        if "an" in taps and s == 0:
            pb.gdma(lambda sy: sy.dma_start(tap_t["an"], AN[:, 0:2048]), wait=(("V", max(an_s)),))
        if upto == "p4":
            stop = True
        if stop:
            continue

        fbar = pb.barrier()
        fbw = [("T", fbar[0]), ("V", fbar[1]), ("S", fbar[2])]
        _o = 0
        def _mk2(n):
            nonlocal _o
            v = TView(ARENA, _o, n)
            _o += n
            return v
        WG = _mk2(NDC * HID)
        WU = _mk2(NDC * HID)
        WD = _mk2(24 * 1024)
        XN2 = _mk2(4096)
        SGB = _mk2(512)
        HCH = _mk2(1024)

        wg_tok, wu_tok, wd_tok = [], [], []
        for hb in range(NHB):
            wg_tok.append(pb.gdma(lambda g, hb=hb: g.dma_start(
                WG[:, :].rearrange("p (c h) -> p c h", h=HID)[:, :, hb * 256:(hb + 1) * 256],
                wgt_v[:, :, hb * 256:(hb + 1) * 256]), wait=fbw if hb == 0 else ()))
            wu_tok.append(pb.gdma(lambda g, hb=hb: g.dma_start(
                WU[:, :].rearrange("p (c h) -> p c h", h=HID)[:, :, hb * 256:(hb + 1) * 256],
                wut_v[:, :, hb * 256:(hb + 1) * 256]), wait=fbw if hb == 0 else ()))
            wd_tok.append(pb.gdma(lambda g, hb=hb: g.dma_start(
                WD[:, :].rearrange("p (c d) -> p c d", d=1024)[:, hb * 2:(hb + 1) * 2, :],
                wdt_v[:, hb * 2:(hb + 1) * 2, :]), wait=fbw if hb == 0 else ()))

        PD = TView(PSUMA, 2048, 2048)
        xn2_v = [0, 0]
        xn2_used_t = [0, 0]
        out_v = 0
        sg_v = 0
        hch_t = 0
        pd_evac_v = 0
        F3 = {"SX": TView(PSUMA, 1536, 256),
              "SXX": TView(PSUMA, 1024, 256),
              "ABCP": TView(PSUMA, 1536, 512)}

        def ln3_steps(sp8):
            lo = sp8 * 256
            lane = sp8 % 2
            steps = ln_stats_steps(
                F3, 256,
                lambda dc, lo=lo: AXv(dc, lo, 256),
                t_src_waits=(fbw if sp8 == 0 else []),
                v_src_waits=(fbw if sp8 == 0 else []))

            def mk_norm(dc):
                def go():
                    w = [("V", pb.cnt("V"))]
                    if dc == 0 and xn2_used_t[lane]:
                        w.append(("T", xn2_used_t[lane]))
                    v1 = pb.emit("vector", lambda v, dc=dc, lo=lo: v.tensor_tensor(
                        SQ[:, 512:768], AXv(dc, lo, 256), ABC[:, 0:256], ALU.mult),
                        wait=w, inc="V")
                    xn2_v[lane] = pb.emit("vector", lambda v, dc=dc, lane=lane: v.tensor_tensor(
                        XN2[:, lane * 2048 + dc * 256:lane * 2048 + (dc + 1) * 256],
                        SQ[:, 512:768], ABC[:, 256:512], ALU.add),
                        wait=(("V", v1),), inc="V")
                    pb.war_v = xn2_v[lane]
                return go
            return steps + [mk_norm(dc) for dc in range(NDC)]

        for f in ln3_steps(0):
            f()
        for sp8 in range(NSP8):
            lo = sp8 * 256
            lane = sp8 % 2
            ln3_q = ln3_steps(sp8 + 1) if sp8 + 1 < NSP8 else []
            if "xn2" in taps and s == 0 and sp8 == 0:
                pb.gdma(lambda sy: sy.dma_start(tap_t["xn2"], XN2[:, 0:2048]), wait=(("V", xn2_v[0]),))
            GU = TView(PSUMA, 0, 1024)
            gu_s = [0, 0]
            gu_v = [0, 0]
            hch_used_t = [0, 0, 0, 0]
            pend_down = []

            def emit_down(hidc, lh, v_h):
                nonlocal hch_t, pd_evac_v
                w = [("V", v_h)]
                if hidc == 0 and pd_evac_v:
                    w.append(("V", pd_evac_v))
                for dco in range(NDC):
                    r = pb.emit("tensor", lambda t, hidc=hidc, dco=dco, lh=lh: t.matmul(
                        PD[:, dco * 256:(dco + 1) * 256],
                        WD[:, hidc * 1024 + dco * 128:hidc * 1024 + (dco + 1) * 128],
                        HCH[:, lh * 256:(lh + 1) * 256],
                        start=(hidc == 0 and dco % 2 == 0), stop=(hidc == 23),
                        skip_group_check=True),
                        wait=w if dco == 0 else (), inc="T" if dco == NDC - 1 else None)
                    if r is not None:
                        hch_t = r

            for hidc in range(24):
                lg = hidc % 2
                hb = hidc // 2
                w = []
                if hidc == 0:
                    w = [("V", xn2_v[lane]), wg_tok[hb], wu_tok[hb], wd_tok[hb]]
                elif hidc % 2 == 0:
                    w = [wg_tok[hb], wu_tok[hb], wd_tok[hb]]
                if gu_s[lg]:
                    w = w + [("S", gu_s[lg])]
                if gu_v[lg]:
                    w = w + [("V", gu_v[lg])]
                for dc in range(NDC):
                    pb.emit("tensor", lambda t, hidc=hidc, dc=dc, lg=lg, lane=lane: t.matmul(
                        GU[:, lg * 512:lg * 512 + 256],
                        WG[:, dc * HID + hidc * 128:dc * HID + (hidc + 1) * 128],
                        XN2[:, lane * 2048 + dc * 256:lane * 2048 + (dc + 1) * 256],
                        start=(dc == 0), stop=(dc == NDC - 1)),
                        wait=w if dc == 0 else ())
                t_gu = 0
                for dc in range(NDC):
                    r = pb.emit("tensor", lambda t, hidc=hidc, dc=dc, lg=lg, lane=lane: t.matmul(
                        GU[:, lg * 512 + 256:lg * 512 + 512],
                        WU[:, dc * HID + hidc * 128:dc * HID + (hidc + 1) * 128],
                        XN2[:, lane * 2048 + dc * 256:lane * 2048 + (dc + 1) * 256],
                        start=(dc == 0), stop=(dc == NDC - 1)),
                        inc="T" if dc == NDC - 1 else None)
                    if r is not None:
                        t_gu = r
                xn2_used_t[lane] = t_gu
                for _ in range(2):
                    if ln3_q and (hidc >= 2 or sp8 == 0):
                        ln3_q.pop(0)()
                wsg = [("T", t_gu)]
                if sg_v:
                    wsg.append(("V", sg_v))
                s_g = pb.emit("scalar", lambda s2, lg=lg: s2.activation(
                    SGB[:, lg * 256:(lg + 1) * 256],
                    GU[:, lg * 512:lg * 512 + 256], AF.Silu, scale=CQI[:, 0:1]),
                    wait=wsg, inc="S")
                gu_s[lg] = s_g
                lh = hidc % 4
                wh = [("S", s_g)]
                if hch_used_t[lh]:
                    wh.append(("T", hch_used_t[lh]))
                v_h = pb.emit("vector", lambda v, lg=lg, lh=lh: v.scalar_tensor_tensor(
                    HCH[:, lh * 256:(lh + 1) * 256],
                    SGB[:, lg * 256:(lg + 1) * 256], CQI[:, 0:1],
                    GU[:, lg * 512 + 256:lg * 512 + 512], ALU.mult, ALU.mult),
                    wait=wh, inc="V")
                gu_v[lg] = v_h
                sg_v = v_h
                pend_down.append((hidc, lh, v_h))
                if hidc >= 2:
                    h2, l2, v2 = pend_down.pop(0)
                    emit_down(h2, l2, v2)
                    hch_used_t[l2] = hch_t
            while pend_down:
                h2, l2, v2 = pend_down.pop(0)
                emit_down(h2, l2, v2)
                hch_used_t[l2] = hch_t
            while ln3_q:
                ln3_q.pop(0)()
            for dco in range(NDC):
                lq_ = dco % 4
                w = [("T", hch_t)] if dco == 0 else []
                if len(out_toks) >= 4:
                    w.append(out_toks[-4])
                out_v = pb.emit("vector", lambda v, dco=dco, lq_=lq_, lo=lo: v.scalar_tensor_tensor(
                    OUTB[:, lq_ * 256:(lq_ + 1) * 256], PD[:, dco * 256:(dco + 1) * 256],
                    CQI[:, 0:1], AXv(dco, lo, 256), ALU.mult, ALU.add),
                    wait=w, inc="V")
                pd_evac_v = out_v
                out_toks.append(pb.ydma(lambda sy, dco=dco, lq_=lq_, lo=lo, s=s: sy.dma_start(
                    yT[dco * 128:(dco + 1) * 128, s * SEQ + lo:s * SEQ + lo + 256],
                    OUTB[:, lq_ * 256:(lq_ + 1) * 256]),
                    wait=(("V", out_v),)))

    for n in ("y0", "y1", "g0", "g1", "g2", "g3"):
        if pb.cnt(n):
            pb.emit("sync", None, wait=((n, pb.cnt(n)),))

    with nc.Block() as block:
        @block.tensor
        def _(e):
            for f in pb.ops["tensor"]:
                f(e)

        @block.vector
        def _(e):
            for f in pb.ops["vector"]:
                f(e)

        @block.scalar
        def _(e):
            for f in pb.ops["scalar"]:
                f(e)

        @block.sync
        def _(e):
            for f in pb.ops["sync"]:
                f(e)

        @block.gpsimd
        def _(e):
            for f in pb.ops["gpsimd"]:
                f(e)

    return nc


def _host_layout(inputs):
    x = np.asarray(inputs["L0_attn_x"], dtype=np.float32)
    w1 = np.asarray(inputs["L0_attn_W1"], dtype=np.float32)
    b1 = np.asarray(inputs["L0_attn_b1"], dtype=np.float32)
    w2 = np.asarray(inputs["L0_attn_W2"], dtype=np.float32)
    b2 = np.asarray(inputs["L0_attn_b2"], dtype=np.float32)
    gam = np.asarray(inputs["L0_attn_gamma"], dtype=np.float32)
    bet = np.asarray(inputs["L0_attn_beta"], dtype=np.float32)
    wg = np.asarray(inputs["L0_ffn_w_gate"], dtype=np.int32)
    wu = np.asarray(inputs["L0_ffn_w_up"], dtype=np.int32)
    wd = np.asarray(inputs["L0_ffn_w_down"], dtype=np.int32)

    w1g = np.concatenate(
        [np.concatenate([w1[:, 2048 + g * 128:2048 + (g + 1) * 128],
                         w1[:, 3072 + g * 128:3072 + (g + 1) * 128],
                         w1[:, 1024 + g * 128:1024 + (g + 1) * 128]], axis=1)
         for g in range(NHG)], axis=1)
    shared = {
        "w1u": np.ascontiguousarray(w1[:, 0:1024]),
        "w1g": np.ascontiguousarray(w1g),
        "w2": np.ascontiguousarray(w2),
        "b1u": np.ascontiguousarray(b1[0:1024].reshape(8, 128).T),
        "b1q": np.ascontiguousarray(b1[2048:3072].reshape(8, 128).T),
        "b1k": np.ascontiguousarray(b1[3072:4096].reshape(8, 128).T),
        "b1vr": np.ascontiguousarray(np.broadcast_to(b1[1024:2048], (128, 1024))),
        "b2l": np.ascontiguousarray(b2.reshape(8, 128).T),
        "gaml": np.ascontiguousarray(gam.reshape(8, 128).T),
        "betl": np.ascontiguousarray(bet.reshape(8, 128).T),
        "wgt": np.ascontiguousarray(wg.T),
        "wut": np.ascontiguousarray(wu.T),
        "wdt": np.ascontiguousarray(wd.T),
    }
    in_maps = []
    for c in range(8):
        m = dict(shared)
        xc = x[2 * c:2 * c + 2]
        m["xT"] = np.ascontiguousarray(xc.transpose(2, 0, 1).reshape(D, NSEQ * SEQ))
        in_maps.append(m)
    return in_maps


def kernel(**inputs):
    in_maps = _host_layout(inputs)
    nc = build_program()
    res = run_bass_kernel_spmd(nc, in_maps, core_ids=list(range(8)))
    return np.concatenate(
        [np.asarray(res.results[c]["yT"]).reshape(D, NSEQ, SEQ).transpose(1, 2, 0)
         for c in range(8)], axis=0)
